# version 15 (final)
# speedup vs baseline: 1.1106x; 1.0087x over previous
import math
import numpy as np
import ml_dtypes
import concourse.bass as bass
import concourse.mybir as mybir
from concourse.bass_utils import run_bass_kernel_spmd

F32 = mybir.dt.float32
BF16 = mybir.dt.bfloat16
Alu = mybir.AluOpType
Act = mybir.ActivationFunctionType
Ax = mybir.AxisListType

H, HKV, D, CONVK = 16, 4, 64, 4
C = 1024
L = 128
SEG = 512
NEG = -1e30

ENGINES = ("pe", "dve", "act", "sp", "gp", "gpc")


class Buf:
    __slots__ = ("name", "last_write", "readers")

    def __init__(self, name):
        self.name = name
        self.last_write = None
        self.readers = []


class OpRec:
    __slots__ = ("idx", "eng", "fn", "deps", "needs_inc", "inc_val", "waits",
                 "drain_upto", "label")

    def __init__(self, idx, eng, fn, deps, label):
        self.idx = idx
        self.eng = eng
        self.fn = fn
        self.deps = deps
        self.needs_inc = False
        self.inc_val = None
        self.waits = {}
        self.drain_upto = -1
        self.label = label


class Sched:

    def __init__(self):
        self.ops = []
        self.last_dma = {}

    def op(self, eng, fn, reads=(), writes=(), label=""):
        reads = tuple(reads)
        writes = tuple(writes)
        excl = tuple(b for b in reads if b.name.startswith("ps") and b not in writes)
        reads = tuple(b for b in reads if not b.name.startswith("ps") or b in writes)
        writes = writes + excl
        deps = set()
        for b in reads:
            if b.last_write is not None:
                deps.add(b.last_write)
        for b in writes:
            if b.last_write is not None:
                deps.add(b.last_write)
            deps.update(b.readers)
        rec = OpRec(len(self.ops), eng, fn, deps, label)
        self.ops.append(rec)
        for b in reads:
            b.readers.append(rec.idx)
        for b in writes:
            b.last_write = rec.idx
            b.readers = []
        return rec

    def dma(self, queue, fn, reads=(), writes=(), label=""):
        rec = self.op(queue, fn, reads, writes, label)
        prev = self.last_dma.get(queue)
        if prev is not None:
            rec.deps.add(prev)
        self.last_dma[queue] = rec.idx
        return rec

    def finalize(self):
        for o in self.ops:
            for d in sorted(o.deps):
                dop = self.ops[d]
                if dop.eng == o.eng and dop.eng in ("dve", "act", "gpc"):
                    o.drain_upto = max(o.drain_upto, d)
                elif dop.eng == o.eng and dop.eng == "pe":
                    pass
                else:
                    dop.needs_inc = True
        ctr = {e: 0 for e in ENGINES}
        for o in self.ops:
            if o.eng in ("sp", "gp"):
                ctr[o.eng] += 16
                o.inc_val = ctr[o.eng]
                o.needs_inc = True
            elif o.needs_inc:
                ctr[o.eng] += 1
                o.inc_val = ctr[o.eng]
        for o in self.ops:
            for d in sorted(o.deps):
                dop = self.ops[d]
                if dop.eng != o.eng or dop.eng in ("sp", "gp"):
                    v = dop.inc_val
                    assert v is not None
                    if o.waits.get(dop.eng, 0) < v:
                        o.waits[dop.eng] = v
        return ctr

    def emit(self, nc, sems, handles):
        totals = {e: 0 for e in ENGINES}
        for o in self.ops:
            if o.inc_val is not None:
                totals[o.eng] = o.inc_val
        streams = {e: [i for i, o in enumerate(self.ops) if o.eng == e]
                   for e in ENGINES}

        def run_stream(eng_name, handle):
            names = (eng_name,) if isinstance(eng_name, str) else tuple(eng_name)
            idxs = sorted(i for n in names for i in streams[n])
            watermark = {e: 0 for e in ENGINES}
            last_drain = -1
            last_emitted = -1
            for i in idxs:
                o = self.ops[i]
                for e, v in sorted(o.waits.items()):
                    if watermark[e] < v:
                        handle.wait_ge(sems[e], v)
                        watermark[e] = v
                if o.drain_upto > last_drain and o.drain_upto <= last_emitted:
                    handle.drain()
                    last_drain = last_emitted
                instr = o.fn(handle)
                if o.label:
                    try:
                        instr.annotate(o.label)
                    except Exception:
                        pass
                last_emitted = i
                if o.inc_val is not None and o.eng not in ("sp", "gp"):
                    instr.then_inc(sems[o.eng], 1)
                elif o.eng in ("sp", "gp"):
                    instr.then_inc(sems[o.eng], 16)
            return watermark

        return streams, totals, run_stream


def ceil_div(a, b):
    return (a + b - 1) // b



def host_constants():
    consts = {}
    consts["iden128"] = np.eye(128, dtype=ml_dtypes.bfloat16)
    i64 = np.eye(64, dtype=ml_dtypes.bfloat16)
    consts["iden64x2"] = np.concatenate([i64, i64], axis=1)
    ar = np.arange(128)
    maskI = (ar[:, None] <= ar[None, :])
    maskS = (ar[:, None] < ar[None, :])
    consts["maskA"] = np.where(np.tile(maskI, (1, 4)), 0.0, NEG).astype(ml_dtypes.bfloat16)
    consts["maskM"] = np.where(maskS, 0.0, NEG).astype(ml_dtypes.bfloat16)
    selq = np.zeros((128, 8, 16), np.float32)
    for b in range(8):
        selq[0:64, b, 2 * b] = 1.0
        selq[64:128, b, 2 * b + 1] = 1.0
    consts["selq"] = selq.reshape(128, 128).astype(ml_dtypes.bfloat16)
    selk = np.zeros((128, 2, 4), np.float32)
    for b in range(2):
        selk[0:64, b, 2 * b] = 1.0
        selk[64:128, b, 2 * b + 1] = 1.0
    consts["selk"] = selk.reshape(128, 8).astype(ml_dtypes.bfloat16)
    Cm = np.zeros((128, 68), np.float32)
    for hk in range(4):
        Cm[hk, 0 + hk] = -1.0
        Cm[32 + hk, 0 + hk] = -0.5
        Cm[hk, 4 + hk] = 1.0
        Cm[68 + hk, 4 + hk] = 1.0
        Cm[32 + hk, 4 + hk] = -0.5
        Cm[68 + hk, 24 + hk] = 1.0
        Cm[hk, 64 + hk] = 1.0
        Cm[32 + hk, 64 + hk] = 0.5
    for h in range(16):
        Cm[h // 4, 8 + h] = 1.0
        Cm[96 + h, 8 + h] = -0.5
    consts["combC"] = Cm
    selr = np.zeros((28, 20, 128), np.float32)
    for r in range(20):
        selr[4 + r, r, :] = 1.0
    consts["selrow"] = selr.reshape(28, 20 * 128).astype(ml_dtypes.bfloat16)
    tr = np.zeros((96, 32), np.float32)
    for r in range(28):
        tr[r, r] = 1.0
    for r in range(4):
        tr[64 + r, 28 + r] = 1.0
    consts["trsel"] = tr
    consts["idenF4"] = np.eye(4, dtype=np.float32)
    selkd = np.zeros((128, 4, 4), np.float32)
    for hk in range(4):
        selkd[0:64, hk, hk] = 1.0
    consts["selkd"] = selkd.reshape(128, 16).astype(ml_dtypes.bfloat16)
    consts["ones1x512"] = np.ones((1, 512), ml_dtypes.bfloat16)
    cp = np.zeros((128, 12), np.float32)
    cp[:, 0] = 1e-5; cp[:, 1] = 1e-12; cp[:, 2] = 1.0; cp[:, 3] = 1.0 / 4096
    cp[:, 4] = -0.5; cp[:, 5] = -1.0; cp[:, 6] = 0.5; cp[:, 7] = np.log(0.125)
    cp[:, 8] = 1.0 / 1024; cp[:, 9] = -1.0 / 1024
    consts["cpool"] = cp
    consts["fones"] = np.ones((128, 128), np.float32)
    consts["onescol"] = np.ones((128, 1), ml_dtypes.bfloat16)
    return consts


def host_layout(inputs, core):
    m = {}
    m["x"] = np.ascontiguousarray(np.asarray(inputs["L0_attn_x"])[core])
    m["wqT"] = np.ascontiguousarray(np.asarray(inputs["L0_attn_Wq"]).T)
    wkT = np.asarray(inputs["L0_attn_Wk"]).T
    cols = []
    for hk in range(4):
        blk = wkT[:, hk * 64:(hk + 1) * 64]
        cols += [blk, blk]
    m["wkT"] = np.ascontiguousarray(np.concatenate(cols, axis=1))
    m["wvT"] = np.ascontiguousarray(np.asarray(inputs["L0_attn_Wv"]).T)
    m["wgT"] = np.ascontiguousarray(np.asarray(inputs["L0_attn_Wgate"]).T)
    m["wpT"] = np.ascontiguousarray(np.asarray(inputs["L0_attn_Wproj"]).T)
    m["wbgT"] = np.ascontiguousarray(
        np.concatenate([np.asarray(inputs["L0_attn_Wgk"]),
                        np.asarray(inputs["L0_attn_Wbeta"])], axis=0).T)
    m["bias_bg"] = np.ascontiguousarray(
        np.concatenate([np.asarray(inputs["L0_attn_dt_bias"]),
                        np.asarray(inputs["L0_attn_b_beta"])])[:, None])
    m["ffn_b"] = np.ascontiguousarray(np.asarray(inputs["L0_ffn_b"])[None, :])
    m["ffn_WT"] = np.ascontiguousarray(np.asarray(inputs["L0_ffn_W"]).T)
    m["lora_downT"] = np.ascontiguousarray(np.asarray(inputs["L0_ffn_lora_down"]).T)
    m["down_aux"] = np.ascontiguousarray(np.asarray(inputs["L0_ffn_down_aux"]))
    m["lora_upT"] = np.ascontiguousarray(np.asarray(inputs["L0_ffn_lora_up"]).T)
    m["up_auxT"] = np.ascontiguousarray(np.asarray(inputs["L0_ffn_up_aux"]).T)
    m["cwq"] = np.ascontiguousarray(np.asarray(inputs["L0_attn_cwq"]))
    cwk = np.asarray(inputs["L0_attn_cwk"])
    m["cwk"] = np.ascontiguousarray(
        np.concatenate(sum([[cwk[hk * 64:(hk + 1) * 64]] * 2 for hk in range(4)], []), axis=0))
    m["cwv"] = np.ascontiguousarray(np.asarray(inputs["L0_attn_cwv"]))
    m["A_log"] = np.ascontiguousarray(np.asarray(inputs["L0_attn_A_log"])[:, None])
    m["rms_w2"] = np.ascontiguousarray(np.tile(np.asarray(inputs["L0_attn_rms_w"]), 2)[:, None])
    for k, v in host_constants().items():
        m[k] = v
    return m



def build_program(T=8192, stage=99):
    nc = bass.Bass("TRN2", target_bir_lowering=False)
    NSEG = T // SEG
    NT = SEG // 128

    dr = {}
    def din(name, shape, dtype=F32):
        dr[name] = nc.dram_tensor(name, shape, dtype, kind="ExternalInput").ap()
        return dr[name]

    din("x", [T, C])
    for n, s in [("wqT", [C, C]), ("wkT", [C, 512]), ("wvT", [C, 256]),
                 ("wgT", [C, C]), ("wpT", [C, C]), ("wbgT", [C, 8]),
                 ("bias_bg", [8, 1]), ("ffn_b", [1, C]), ("ffn_WT", [C, C]),
                 ("lora_downT", [100, 4]), ("down_aux", [100, C]),
                 ("lora_upT", [4, 50]), ("up_auxT", [50, C]),
                 ("cwq", [C, 4]), ("cwk", [512, 4]), ("cwv", [256, 4]),
                 ("A_log", [4, 1]), ("rms_w2", [128, 1]),
                 ("combC", [128, 68]), ("fones", [128, 128]),
                 ("trsel", [96, 32]), ("idenF4", [4, 4]),
                 ("cpool", [128, 12])]:
        din(n, s)
    for n, s in [("iden128", [128, 128]), ("iden64x2", [64, 128]),
                 ("selq", [128, 128]), ("selk", [128, 8]),
                 ("ones1x512", [1, 512]), ("selrow", [28, 20 * 128]),
                 ("selkd", [128, 16]), ("maskA", [128, 512]), ("maskM", [128, 128]),
                 ("onescol", [128, 1])]:
        din(n, s, BF16)
    out_d = nc.dram_tensor("out", [T, C], F32, kind="ExternalOutput").ap()
    dbg_q = dbg_k = dbg_v = dbg_r = dbg_s = dbg_o = None
    if stage == 3:
        dbg_o = nc.dram_tensor("dbg_o", [T, C], F32, kind="ExternalOutput").ap()
    if stage == 2:
        NSEG_ = T // SEG
        dbg_q = nc.dram_tensor("dbg_q", [NSEG_ * 128, 8 * SEG], BF16, kind="ExternalOutput").ap()
        dbg_k = nc.dram_tensor("dbg_k", [NSEG_ * 128, 4 * SEG], BF16, kind="ExternalOutput").ap()
        dbg_v = nc.dram_tensor("dbg_v", [NSEG_ * 128, 2 * SEG], BF16, kind="ExternalOutput").ap()
        dbg_r = nc.dram_tensor("dbg_r", [NSEG_ * 68, SEG], F32, kind="ExternalOutput").ap()
        dbg_s = nc.dram_tensor("dbg_s", [NSEG_ * 128, 4 * C], BF16, kind="ExternalOutput").ap()

    import contextlib
    es = contextlib.ExitStack()
    S = Sched()

    def sb(name, shape, dtype=F32):
        return es.enter_context(nc.sbuf_tensor(name, shape, dtype))

    def ps(name, shape, dtype=F32):
        return es.enter_context(nc.psum_tensor(name, shape, dtype))

    with es:
        wfT = sb("wfT", [128, 8 * C], BF16)
        wqT = sb("wqT_s", [128, 8 * C], BF16)
        wgT = sb("wgT_s", [128, 8 * C], BF16)
        wpT = sb("wpT_s", [128, 8 * C], BF16)
        wkT = sb("wkT_s", [128, 8 * 512], BF16)
        wvT = sb("wvT_s", [128, 8 * 256], BF16)
        wbgT = sb("wbgT_s", [128, 8 * 8], BF16)
        bbg_col = sb("bbg_col", [8, 1], F32)
        bf_row = sb("bf_row", [1, C], BF16)
        iden = sb("iden", [128, 128], BF16)
        iden64x2 = sb("iden64x2_s", [64, 128], BF16)
        selq = sb("selq_s", [128, 128], BF16)
        ones1 = sb("ones1", [1, 512], BF16)
        maskA = sb("maskA_s", [128, 512], BF16)
        maskM = sb("maskM_s", [128, 128], BF16)
        combC = sb("combC_s", [128, 68], F32)
        trsel = sb("trsel_s", [96, 32], F32)
        idenF4 = sb("idenF4_s", [4, 4], F32)
        selrow = sb("selrow_s", [28, 20 * 128], BF16)
        selkd = sb("selkd_s", [128, 16], BF16)
        onescol = sb("onescol_s", [128, 1], BF16)
        cpool = sb("cpool_s", [128, 12], F32)
        fones = sb("fones_s", [128, 128], F32)
        cw = sb("cw", [128, 56], F32)
        negA = sb("negA", [36, 1], F32)
        rmsc = sb("rmsc", [128, 1], F32)

        xseg = [sb(f"xseg{i}", [128, NT * C], F32) for i in range(2)]
        hseg_ = [sb(f"hseg{i}", [128, C], BF16) for i in range(2)]
        hseg = hseg_[0]
        sqbuf = sb("sqbuf", [128, C], BF16)
        stats = sb("stats", [128, 16], F32)
        wbar = sb("wbar", [1, C], BF16)
        lnrb = [sb(f"lnrb{i}", [1, 256], BF16) for i in range(2)]
        lnrf = [sb(f"lnrf{i}", [1, 256], F32) for i in range(2)]
        hf_ = [sb(f"hfr{i}", [128, 8 * 128], BF16) for i in range(2)]
        hf = hf_[0]
        ogf = hf_[0]
        h2f = sb("h2f", [128, 8 * 512], BF16)
        qraw = sb("qraw", [128, 8 * (SEG + 3)], BF16)
        kraw = sb("kraw", [128, 4 * (SEG + 3)], BF16)
        vraw = sb("vraw", [128, 2 * (SEG + 3)], BF16)
        qfm = sb("qfm", [128, 8 * SEG], BF16)
        kfm = sb("kfm", [128, 4 * SEG], BF16)
        vfm = sb("vfm", [128, 2 * SEG], BF16)
        srows = sb("srows", [128, 2 * SEG], F32)
        bgrows = srows[0:8, 0:SEG]
        ab8 = srows[0:8, SEG:2 * SEG]
        srcrows = sb("srcrows", [128, SEG], F32)
        rows2 = sb("rows2", [68, SEG], F32)
        rows2h = sb("rows2h", [28, SEG], BF16)
        rows2l = sb("rows2l", [28, SEG], BF16)
        epack_ = [sb("epack", [96, 128], F32)] * 2
        colpk_ = [sb(f"colpk{i}", [128, 32], F32) for i in range(2)]
        gLrow_ = [sb(f"gLrow{i}", [1, 4], BF16) for i in range(2)]
        gLbc_ = [sb(f"gLbc{i}", [128, 4], F32) for i in range(2)]
        Mpow_ = [sb(f"Mpow{i}", [128, 11 * 128], BF16) for i in range(2)]
        Ttb_ = [sb(f"Ttb{i}", [128, 128], BF16) for i in range(2)]
        P0_ = [sb(f"P0{i}", [128, 128], BF16) for i in range(2)]
        Atb_ = [sb(f"Atb{i}", [128, 512], BF16) for i in range(2)]
        decb_ = [sb("decb", [128, 512], BF16)] * 2
        Mtb_ = [sb(f"Mtb{i}", [128, 128], BF16) for i in range(2)]
        Ub_ = [sb(f"Ub{i}", [128, 64], BF16) for i in range(2)]
        Wb_ = [sb(f"Wb{i}", [128, 64], BF16) for i in range(2)]
        Rw_ = [sb(f"Rw{i}", [128, 64], F32) for i in range(2)]
        Pc_ = [[sb(f"Pc{i}_0", [128, 128], BF16), P0_[i]] for i in range(2)]
        ktm2 = sb("ktm2", [128, 128], BF16)
        Sf32 = sb("Sf32", [128, 4 * 64], F32)
        Sbf = sb("Sbf", [128, 4 * 64], BF16)
        ointer = sb("ointer", [128, 256], BF16)
        obuf = sb("obuf", [128, C], BF16)
        oinv = sb("oinv", [128, 16], F32)
        gbuf = sb("gbuf", [128, C], BF16)
        sgate_ = [sb(f"sgate{i}", [128, C], BF16) for i in range(2)]
        ebuf = sb("ebuf", [128, C], BF16)
        ogbuf = sb("ogbuf", [128, C], BF16)
        psA = ps("psA", [128, 512])
        psB = ps("psB", [128, 512])
        psC = ps("psC", [128, 512])
        psD = ps("psD", [128, 512])
        psE = ps("psE", [128, 512])
        psF = ps("psF", [128, 512])
        psG = ps("psG", [128, 512])
        psH = ps("psH", [128, 512])

        sems = {}
        for e in ENGINES:
            sems[e] = es.enter_context(nc.semaphore(f"sem_{e}"))

        B = {}
        def buf(name):
            if name not in B:
                B[name] = Buf(name)
            return B[name]

        def cast_dma(dst_ap, src_ap, wbuf, label):
            S.dma("gp", lambda g, d=dst_ap, s=src_ap: g.dma_start(out=d, in_=s),
                  writes=(wbuf,), label=label)

        for name, sbuf_t, width in [("wqT", wqT, C), ("wgT", wgT, C),
                                    ("wkT", wkT, 512), ("wvT", wvT, 256),
                                    ("wbgT", wbgT, 8)]:
            srcap = dr[name].rearrange("(b p) w -> p b w", p=128)
            dstap = sbuf_t[:, :].rearrange("p (b w) -> p b w", b=8)
            cast_dma(dstap, srcap, buf(name), f"load_{name}")
        S.dma("sp", lambda s_: s_.dma_start(out=bbg_col[:, :], in_=dr["bias_bg"][:, :]),
              writes=(buf("bbg"),), label="load_bbg")
        for name, sbuf_t in [("iden128", iden), ("iden64x2", iden64x2),
                             ("selq", selq), ("ones1x512", ones1),
                             ("selrow", selrow), ("selkd", selkd), ("onescol", onescol)]:
            S.dma("sp", lambda s_, d=sbuf_t, src=dr[name]: s_.dma_start(out=d[:, :], in_=src[:, :]),
                  writes=(buf(name),), label=f"load_{name}")
        for name, sbuf_t in [("maskA", maskA), ("maskM", maskM),
                             ("combC", combC),
                             ("cpool", cpool), ("trsel", trsel), ("idenF4", idenF4), ("fones", fones),
                             ("A_log", None), ("rms_w2", rmsc)]:
            if name == "A_log":
                S.dma("sp", lambda s_: s_.dma_start(out=negA[32:36, :], in_=dr["A_log"][:, :]),
                      writes=(buf("negA"),), label="load_Alog")
            else:
                S.dma("sp", lambda s_, d=sbuf_t, src=dr[name]: s_.dma_start(out=d[:, :], in_=src[:, :]),
                      writes=(buf(name),), label=f"load_{name}")
        S.dma("sp", lambda s_: s_.dma_start(
            out=cw[:, 0:32].rearrange("p (b k) -> p b k", b=8), in_=dr["cwq"].rearrange("(b p) k -> p b k", p=128)),
            writes=(buf("cw"),), label="load_cwq")
        S.dma("sp", lambda s_: s_.dma_start(
            out=cw[:, 32:48].rearrange("p (b k) -> p b k", b=4), in_=dr["cwk"].rearrange("(b p) k -> p b k", p=128)),
            writes=(buf("cw"),), label="load_cwk")
        S.dma("sp", lambda s_: s_.dma_start(
            out=cw[:, 48:56].rearrange("p (b k) -> p b k", b=2), in_=dr["cwv"].rearrange("(b p) k -> p b k", p=128)),
            writes=(buf("cw"),), label="load_cwv")
        S.op("act", lambda a: a.activation(out=negA[32:36, :], in_=negA[32:36, :], func=Act.Exp),
             reads=(buf("negA"),), writes=(buf("negA"),), label="negA_exp")
        S.op("dve", lambda v: v.tensor_scalar_mul(out=negA[32:36, :], in0=negA[32:36, :], scalar1=-1.0),
             reads=(buf("negA"),), writes=(buf("negA"),), label="negA_neg")

        lt1 = qfm[0:100, 0:1024]
        lt2 = qfm[0:100, 1024:1028]
        lt3 = qfm[0:4, 2048:3072]
        lt4 = qfm[0:4, 1028:1078]
        lt5 = ogbuf[0:50, 0:1024]
        lt6 = qfm[0:50, 3072:4096]
        cast_dma(lt1, dr["down_aux"][:, :], buf("qfm"), "load_downaux")
        cast_dma(lt2, dr["lora_downT"][:, :], buf("qfm"), "load_ldT")
        cast_dma(lt4, dr["lora_upT"][:, :], buf("qfm"), "load_luT")
        cast_dma(lt6, dr["up_auxT"][:, :], buf("qfm"), "load_uaT")
        for nh in range(2):
            S.op("pe", lambda t, nh=nh: t.matmul(
                out=psH[0:4, 0:512], lhsT=lt2,
                rhs=lt1[:, nh * 512:(nh + 1) * 512], start=True, stop=True),
                reads=(buf("qfm"), buf("qfm")), writes=(buf("psH"),), label="l1mm")
            S.op("dve", lambda v, nh=nh: v.tensor_copy(
                out=lt3[:, nh * 512:(nh + 1) * 512], in_=psH[0:4, 0:512]),
                reads=(buf("psH"),), writes=(buf("qfm"),), label="l1evac")
        for nh in range(2):
            S.op("pe", lambda t, nh=nh: t.matmul(
                out=psH[0:50, 0:512], lhsT=lt4,
                rhs=lt3[:, nh * 512:(nh + 1) * 512], start=True, stop=True),
                reads=(buf("qfm"), buf("qfm")), writes=(buf("psH"),), label="l2mm")
            S.op("dve", lambda v, nh=nh: v.tensor_copy(
                out=lt5[:, nh * 512:(nh + 1) * 512], in_=psH[0:50, 0:512]),
                reads=(buf("psH"),), writes=(buf("ogbuf"),), label="l2evac")
        for cb in range(8):
            wt_scratch = xseg[0]
            S.dma("sp", lambda s_, cb=cb: s_.dma_start(
                out=wt_scratch[:, 0:1024], in_=dr["ffn_WT"][cb * 128:(cb + 1) * 128, :]),
                writes=(buf("xseg0"),), label=f"load_WT{cb}")
            for nh in range(2):
                S.op("pe", lambda t, cb=cb, nh=nh: t.matmul(
                    out=(psF if nh == 0 else psG)[:, 0:512],
                    lhsT=lt5[:, cb * 128:(cb + 1) * 128],
                    rhs=lt6[:, nh * 512:(nh + 1) * 512], start=True, stop=True),
                    reads=(buf("ogbuf"), buf("qfm")), writes=(buf("psF") if nh == 0 else buf("psG"),),
                    label=f"l3mm{cb}_{nh}")
            for nh in range(2):
                S.op("dve", lambda v, cb=cb, nh=nh: v.scalar_tensor_tensor(
                    out=wfT[:, cb * 1024 + nh * 512: cb * 1024 + (nh + 1) * 512],
                    in0=wt_scratch[:, nh * 512:(nh + 1) * 512], scalar=1.0,
                    in1=(psF if nh == 0 else psG)[:, 0:512],
                    op0=Alu.mult, op1=Alu.add),
                    reads=(buf("xseg0"), buf("psF") if nh == 0 else buf("psG")), writes=(buf("wfT"),),
                    label=f"weff{cb}_{nh}")
        for cb in range(8):
            S.op("dve", lambda v, cb=cb: v.tensor_scalar_mul(
                out=wfT[:, cb * 1024:(cb + 1) * 1024],
                in0=wfT[:, cb * 1024:(cb + 1) * 1024], scalar1=0.5),
                reads=(buf("wfT"),), writes=(buf("wfT"),), label=f"wfT_half{cb}")
        S.dma("sp", lambda s_: s_.dma_start(out=xseg[0][0:1, 0:1024], in_=dr["ffn_b"][:, :]),
              writes=(buf("xseg0"),), label="load_ffnb")
        S.op("dve", lambda v: v.tensor_scalar_mul(out=bf_row[:, :], in0=xseg[0][0:1, 0:1024], scalar1=0.5),
             reads=(buf("xseg0"),), writes=(buf("bf_row"),), label="bf_half")
        for nh in range(2):
            for cb in range(8):
                S.op("pe", lambda t_, nh=nh, cb=cb: t_.matmul(
                    out=psH[0:1, 0:512], lhsT=onescol[:, :],
                    rhs=wfT[:, cb * 1024 + nh * 512: cb * 1024 + (nh + 1) * 512],
                    start=(cb == 0), stop=(cb == 7)),
                    reads=(buf("onescol"), buf("wfT")), writes=(buf("psH"),),
                    label=f"wbar_{nh}_{cb}")
            S.op("act", lambda a, nh=nh: a.activation(
                out=wbar[0:1, nh * 512:(nh + 1) * 512], in_=psH[0:1, 0:512],
                func=Act.Copy),
                reads=(buf("psH"),), writes=(buf("wbar"),), label=f"wbarev_{nh}")
        for cb in range(8):
            S.dma("sp", lambda s_, cb=cb: s_.dma_start(
                out=xseg[0][:, 0:1024], in_=dr["wpT"][cb * 128:(cb + 1) * 128, :]),
                writes=(buf("xseg0"),), label=f"load_wpT{cb}")
            S.op("dve", lambda v, cb=cb: v.tensor_scalar_mul(
                out=wpT[:, cb * 1024:(cb + 1) * 1024], in0=xseg[0][:, 0:1024],
                scalar1=rmsc[:, :]),
                reads=(buf("xseg0"), buf("rms_w2")), writes=(buf("wpT"),),
                label=f"wpT_fold{cb}")

        PAD = SEG + 3

        def tile_x(seg, t):
            return xseg[seg % 2][:, t * C:(t + 1) * C]

        def layernorm(seg, t, xb, tag):
            lp = t % 2
            hsg = hseg_[lp]
            bhs = f"hseg{lp}"
            cb0 = 2 + lp * 7
            def sc(i):
                return stats[:, cb0 + i:cb0 + i + 1]
            def sb_(i):
                return buf(f"lnst{lp}_{i}")
            xt = tile_x(seg, t)
            S.op("act", lambda a: a.activation(out=sqbuf[:, :], in_=xt, func=Act.Square),
                 reads=(xb,), writes=(buf("sqbuf"),), label=f"ln_sq_{tag}")
            S.op("dve", lambda v: v.tensor_reduce(
                out=sc(0), in_=xt, axis=Ax.X, op=Alu.add),
                reads=(xb,), writes=(sb_(0),), label=f"ln_sum_{tag}")
            S.op("dve", lambda v: v.tensor_reduce(
                out=sc(1), in_=sqbuf[:, :], axis=Ax.X, op=Alu.add),
                reads=(buf("sqbuf"),), writes=(sb_(1),), label=f"ln_ssq_{tag}")
            S.op("act", lambda a: a.activation(
                out=sc(2), in_=sc(0), func=Act.Copy, scale=cpool[:, 8:9]),
                reads=(sb_(0), buf("cpool")), writes=(sb_(2),), label=f"ln_mean_{tag}")
            S.op("act", lambda a: a.activation(
                out=sc(3), in_=sc(1), func=Act.Copy, scale=cpool[:, 9:10]),
                reads=(sb_(1), buf("cpool")), writes=(sb_(3),), label=f"ln_msq_{tag}")
            S.op("dve", lambda v: v.scalar_tensor_tensor(
                out=sc(4), in0=sc(2), scalar=sc(2),
                in1=sc(3), op0=Alu.mult, op1=Alu.add),
                reads=(sb_(2), sb_(3)), writes=(sb_(4),),
                label=f"ln_negvar_{tag}")
            S.op("act", lambda a: a.activation(
                out=sc(5), in_=sc(4), func=Act.Ln,
                scale=cpool[:, 5:6], bias=cpool[:, 0:1]),
                reads=(sb_(4), buf("cpool")), writes=(sb_(5),),
                label=f"ln_ln_{tag}")
            S.op("act", lambda a: a.activation(
                out=sc(6), in_=sc(5), func=Act.Exp,
                scale=cpool[:, 4:5]),
                reads=(sb_(5), buf("cpool")), writes=(sb_(6),),
                label=f"ln_inv_{tag}")
            S.op("dve", lambda v: v.tensor_scalar(
                out=hsg[:, :], in0=xt, scalar1=sc(2), scalar2=sc(6),
                op0=Alu.subtract, op1=Alu.mult),
                reads=(xb, sb_(2), sb_(6)), writes=(buf(bhs),),
                label=f"ln_norm_{tag}")
            return hsg, bhs

        def transpose_src(src_sb, src_buf, dst_ap_fn, dst_buf, tag):
            for half in range(2):
                for b in range(4):
                    cb = half * 4 + b
                    S.op("pe", lambda t_, cb=cb, b=b: t_.matmul(
                        out=psH[:, b * 128:(b + 1) * 128],
                        lhsT=src_sb[:, cb * 128:(cb + 1) * 128], rhs=iden[:, :],
                        start=True, stop=True),
                        reads=(src_buf, buf("iden128")), writes=(buf("psH"),),
                        label=f"tr_{tag}_{cb}")
                S.op("act", lambda a, half=half: a.activation(
                    out=dst_ap_fn(half), in_=psH[:, 0:512], func=Act.Copy),
                    reads=(buf("psH"),), writes=(dst_buf,),
                    label=f"trev_{tag}_{half}")

        def ffn_gen(seg, t, xb, tag, sps=None, spsb="psE", single=False):
            sps = psE if sps is None else sps
            p_ = t % 2
            hsegp, hfp = hseg_[p_], hf_[p_]
            bh, bf_ = f"hseg{p_}", f"hfr{p_}"
            lb, lf = lnrb[p_], lnrf[p_]
            blb, blf = f"lnrb{p_}", f"lnrf{p_}"
            xt = tile_x(seg, t)
            S.op("dve", lambda v: v.tensor_copy(out=hsegp[:, :], in_=xt),
                 reads=(xb,), writes=(buf(bh),), label=f"cast_{tag}_{seg}_{t}")
            yield
            transpose_src(hsegp, buf(bh),
                          lambda half: hfp[:, half * 512:(half + 1) * 512],
                          buf(bf_), f"{tag}{seg}_{t}")
            yield
            S.op("act", lambda a: a.activation(out=sqbuf[:, :], in_=hfp[:, :],
                                               func=Act.Square),
                 reads=(buf(bf_),), writes=(buf("sqbuf"),), label=f"fsq_{tag}_{seg}_{t}")
            for cb in range(8):
                S.op("pe", lambda t_, cb=cb: t_.matmul(
                    out=sps[0:1, 256:384], lhsT=onescol[:, :],
                    rhs=hfp[:, cb * 128:(cb + 1) * 128],
                    start=(cb == 0), stop=(cb == 7)),
                    reads=(buf(bf_), buf("onescol")), writes=(buf(spsb),),
                    label=f"fsum_{tag}_{seg}_{t}_{cb}")
            for cb in range(8):
                S.op("pe", lambda t_, cb=cb: t_.matmul(
                    out=sps[0:1, 384:512], lhsT=onescol[:, :],
                    rhs=sqbuf[:, cb * 128:(cb + 1) * 128],
                    start=(cb == 0), stop=(cb == 7)),
                    reads=(buf("sqbuf"), buf("onescol")), writes=(buf(spsb),),
                    label=f"fssq_{tag}_{seg}_{t}_{cb}")
            S.op("act", lambda a, lb=lb: a.activation(
                out=lb[0:1, 0:128], in_=sps[0:1, 256:384], func=Act.Copy,
                scale=cpool[0:1, 9:10]),
                reads=(buf(spsb), buf("cpool")), writes=(buf(blb),),
                label=f"fnegm_{tag}_{seg}_{t}")
            S.op("act", lambda a, lf=lf: a.activation(
                out=lf[0:1, 0:128], in_=sps[0:1, 384:512], func=Act.Copy,
                scale=cpool[0:1, 8:9]),
                reads=(buf(spsb), buf("cpool")), writes=(buf(blf),),
                label=f"fmsq_{tag}_{seg}_{t}")
            yield
            S.op("dve", lambda v, lb=lb, lf=lf: v.tensor_tensor(
                out=lf[0:1, 128:256], in0=lb[0:1, 0:128], in1=lb[0:1, 0:128],
                op=Alu.mult),
                reads=(buf(blb),), writes=(buf(blf),), label=f"fnm2_{tag}_{seg}_{t}")
            yield
            S.op("dve", lambda v, lf=lf: v.tensor_tensor(
                out=lf[0:1, 0:128], in0=lf[0:1, 0:128], in1=lf[0:1, 128:256],
                op=Alu.subtract),
                reads=(buf(blf),), writes=(buf(blf),), label=f"fvar_{tag}_{seg}_{t}")
            yield
            S.op("act", lambda a, lf=lf: a.activation(
                out=lf[0:1, 0:128], in_=lf[0:1, 0:128], func=Act.Ln,
                bias=cpool[0:1, 0:1]),
                reads=(buf(blf), buf("cpool")), writes=(buf(blf),),
                label=f"flnv_{tag}_{seg}_{t}")
            yield
            S.op("act", lambda a, lf=lf, lb=lb: a.activation(
                out=lb[0:1, 128:256], in_=lf[0:1, 0:128], func=Act.Exp,
                scale=cpool[0:1, 4:5]),
                reads=(buf(blf), buf("cpool")), writes=(buf(blb),),
                label=f"finv_{tag}_{seg}_{t}")
            yield
            S.op("pe", lambda t_, lb=lb: t_.matmul(
                out=sps[:, 240:241], lhsT=lb[0:1, 128:256], rhs=ones1[0:1, 0:1],
                start=True, stop=True),
                reads=(buf(blb), buf("ones1x512")), writes=(buf(spsb),),
                label=f"ficol_{tag}_{seg}_{t}")
            S.op("dve", lambda v: v.tensor_copy(
                out=stats[:, p_:p_ + 1], in_=sps[:, 240:241]),
                reads=(buf(spsb),), writes=(buf(f"st_{p_}"),),
                label=f"ficev_{tag}_{seg}_{t}")
            yield
            for nh in range(2):
                pp = psF if (single or nh == 0) else psG
                pb = buf("psF") if (single or nh == 0) else buf("psG")
                for cb in range(8):
                    S.op("pe", lambda t_, cb=cb, nh=nh, pp=pp: t_.matmul(
                        out=pp[:, 0:512], lhsT=hfp[:, cb * 128:(cb + 1) * 128],
                        rhs=wfT[:, cb * 1024 + nh * 512: cb * 1024 + (nh + 1) * 512],
                        start=(cb == 0), stop=False),
                        reads=(buf(bf_), buf("wfT")), writes=(pb,),
                        label=f"ffn_{tag}_{seg}_{t}_{nh}_{cb}")
                S.op("pe", lambda t_, nh=nh, pp=pp: t_.matmul(
                    out=pp[:, 0:512], lhsT=ones1[0:1, 0:128],
                    rhs=bf_row[:, nh * 512:(nh + 1) * 512], start=False, stop=False),
                    reads=(buf("ones1x512"), buf("bf_row")), writes=(pb,),
                    label=f"ffnb_{tag}_{seg}_{t}_{nh}")
                S.op("pe", lambda t_, nh=nh, pp=pp, lb=lb: t_.matmul(
                    out=pp[:, 0:512], lhsT=lb[0:1, 0:128],
                    rhs=wbar[0:1, nh * 512:(nh + 1) * 512], start=False, stop=True),
                    reads=(buf(blb), buf("wbar")), writes=(pb,),
                    label=f"ffnm_{tag}_{seg}_{t}_{nh}")
                S.op("dve", lambda v, nh=nh, pp=pp, xt=xt: v.scalar_tensor_tensor(
                    out=xt[:, nh * 512:(nh + 1) * 512], in0=pp[:, 0:512],
                    scalar=stats[:, p_:p_ + 1],
                    in1=xt[:, nh * 512:(nh + 1) * 512],
                    op0=Alu.mult, op1=Alu.add),
                    reads=(pb, xb, buf(f"st_{p_}")), writes=(xb,),
                    label=f"ffnev_{tag}_{seg}_{t}_{nh}")
            yield

        def run_pair(gens):
            alive = [True] * len(gens)
            while any(alive):
                for gi in range(len(gens)):
                    if alive[gi]:
                        try:
                            next(gens[gi])
                        except StopIteration:
                            alive[gi] = False

        for bname, t_sb in [("qraw", qraw), ("kraw", kraw), ("vraw", vraw)]:
            S.op("dve", lambda v, t_sb=t_sb: v.memset(t_sb[:, :], 0.0),
                 writes=(buf(bname),), label=f"memset_{bname}")
        S.op("dve", lambda v: v.memset(srcrows[:, :], 0.0),
             writes=(buf("srcrows"),), label="memset_srcrows")
        S.op("dve", lambda v: v.memset(epack_[0][:, :], 0.0),
             writes=(buf("epack"),), label="memset_epack")
        S.op("dve", lambda v: v.memset(Sf32[:, :], 0.0),
             writes=(buf("Sf32"),), label="memset_Sf32")
        S.op("dve", lambda v: v.memset(Sbf[:, :], 0.0),
             writes=(buf("Sbf"),), label="memset_Sbf")

        PROJ = [
            ("q", qraw, "qraw", qfm, "qfm", wqT, "wqT", 8, 1024, 0),
            ("k", kraw, "kraw", kfm, "kfm", wkT, "wkT", 4, 512, 32),
            ("v", vraw, "vraw", vfm, "vfm", wvT, "wvT", 2, 256, 48),
        ]

        xbufs = [buf("xseg0"), buf("xseg1")]
        src_x = dr["x"].rearrange("(s t p) c -> s p t c", p=128, t=NT)
        dst_x = out_d.rearrange("(s t p) c -> s p t c", p=128, t=NT)
        for seg in range(NSEG):
            xb = xbufs[seg % 2]
            if seg == 0 or stage == 1:
                S.dma("sp", lambda s_, seg=seg: s_.dma_start(
                    out=xseg[seg % 2][:, :].rearrange("p (t c) -> p t c", t=NT),
                    in_=src_x[seg]),
                    writes=(xb,), label=f"xload_{seg}")
            if seg == 0 or stage == 1:
                for tp in range(0, NT, 2):
                    run_pair([ffn_gen(seg, tp, xb, "f1"), ffn_gen(seg, tp + 1, xb, "f1")])
            if stage == 1:
                S.dma("gp", lambda g, seg=seg: g.dma_start(
                    out=dst_x[seg], in_=xseg[seg % 2][:, :].rearrange("p (t c) -> p t c", t=NT)),
                    reads=(xb,), label=f"xstore_{seg}")
                continue

            h2f_v = h2f[:, :].rearrange("p (cb t w) -> p cb t w", t=NT, w=128)
            lnq = []
            for t in range(NT):
                lnq.append(layernorm(seg, t, xb, f"l2_{seg}_{t}"))
                if t > 0:
                    hsg_, bhs_ = lnq[t - 1]
                    transpose_src(hsg_, buf(bhs_),
                                  lambda half, t=t - 1, h2f_v=h2f_v: h2f_v[:, half * 4:(half + 1) * 4, t, :],
                                  buf("h2f"), f"h2_{seg}_{t - 1}")
            hsg_, bhs_ = lnq[NT - 1]
            transpose_src(hsg_, buf(bhs_),
                          lambda half, t=NT - 1, h2f_v=h2f_v: h2f_v[:, half * 4:(half + 1) * 4, t, :],
                          buf("h2f"), f"h2_{seg}_{NT - 1}")

            for name, raw_sb, raw_b, _, _, _, _, nmb, _, _ in PROJ:
                for mb in range(nmb):
                    S.op("dve", lambda g, raw_sb=raw_sb, mb=mb: g.tensor_copy(
                        out=raw_sb[:, mb * PAD: mb * PAD + 3],
                        in_=raw_sb[:, mb * PAD + SEG: mb * PAD + SEG + 3]),
                        reads=(buf(raw_b),), writes=(buf(raw_b + "c"),),
                        label=f"carry_{name}_{seg}_{mb}")

            pbank = [0]

            def proj_gen(name, raw_sb, raw_b, fm_sb, fm_b, wT_sb, wT_b, nmb,
                         width, cwo, sel_sb, sel_b, nsel, ssq_ps, ssq_psb, ssq_rows):
                for mb in range(nmb):
                    bk = pbank[0] % 2
                    pbank[0] += 1
                    pp = psA if bk == 0 else psB
                    pbn = "psA" if bk == 0 else "psB"
                    for cb in range(8):
                        S.op("pe", lambda t_, cb=cb, mb=mb, pp=pp: t_.matmul(
                            out=pp[:, 0:512],
                            lhsT=wT_sb[:, cb * width + mb * 128: cb * width + (mb + 1) * 128],
                            rhs=h2f[:, cb * 512:(cb + 1) * 512],
                            start=(cb == 0), stop=(cb == 7)),
                            reads=(buf(wT_b), buf("h2f")), writes=(buf(pbn),),
                            label=f"proj_{name}_{seg}_{mb}_{cb}")
                    S.op("act", lambda a, mb=mb, pp=pp: a.activation(
                        out=raw_sb[:, mb * PAD + 3: mb * PAD + 3 + SEG],
                        in_=pp[:, 0:512], func=Act.Copy),
                        reads=(buf(pbn), buf(raw_b + "c")), writes=(buf(raw_b),),
                        label=f"projev_{name}_{seg}_{mb}")
                    yield
                    S.op("dve", lambda v, mb=mb: v.tensor_scalar_mul(
                        out=fm_sb[:, mb * SEG:(mb + 1) * SEG],
                        in0=raw_sb[:, mb * PAD: mb * PAD + SEG],
                        scalar1=cw[:, cwo + mb * 4: cwo + mb * 4 + 1]),
                        reads=(buf(raw_b), buf("cw")), writes=(buf(fm_b),),
                        label=f"conv_{name}_{seg}_{mb}_0")
                    for j in range(1, 4):
                        S.op("dve", lambda v, mb=mb, j=j: v.scalar_tensor_tensor(
                            out=fm_sb[:, mb * SEG:(mb + 1) * SEG],
                            in0=raw_sb[:, mb * PAD + j: mb * PAD + j + SEG],
                            scalar=cw[:, cwo + mb * 4 + j: cwo + mb * 4 + j + 1],
                            in1=fm_sb[:, mb * SEG:(mb + 1) * SEG],
                            op0=Alu.mult, op1=Alu.add),
                            reads=(buf(raw_b), buf("cw"), buf(fm_b)), writes=(buf(fm_b),),
                            label=f"conv_{name}_{seg}_{mb}_{j}")
                    yield
                    if sel_sb is not None and name == "q":
                        S.op("act", lambda a, mb=mb: a.activation(
                            out=hseg_[0][:, (mb % 2) * 512:(mb % 2) * 512 + 512],
                            in_=fm_sb[:, mb * SEG:(mb + 1) * SEG],
                            func=Act.Square),
                            reads=(buf(fm_b),), writes=(buf("hseg0"),),
                            label=f"ssq_{name}_{seg}_{mb}")
                        if mb > 0:
                            S.op("pe", lambda t_, mb=mb: t_.matmul(
                                out=ssq_ps, lhsT=sel_sb[:, (mb - 1) * nsel:mb * nsel],
                                rhs=hseg_[0][:, ((mb - 1) % 2) * 512:((mb - 1) % 2) * 512 + 512],
                                start=(mb - 1 == 0), stop=False),
                                reads=(buf("hseg0"), buf(sel_b)), writes=(buf(ssq_psb),),
                                label=f"ssqmm_{name}_{seg}_{mb - 1}")
                        yield
                        if mb == nmb - 1:
                            S.op("pe", lambda t_, mb=mb: t_.matmul(
                                out=ssq_ps, lhsT=sel_sb[:, mb * nsel:(mb + 1) * nsel],
                                rhs=hseg_[0][:, (mb % 2) * 512:(mb % 2) * 512 + 512],
                                start=False, stop=True),
                                reads=(buf("hseg0"), buf(sel_b)), writes=(buf(ssq_psb),),
                                label=f"ssqmm_{name}_{seg}_{mb}")
                            yield
                    elif sel_sb is not None:
                        S.op("act", lambda a, mb=mb: a.activation(
                            out=sqbuf[:, 0:512], in_=fm_sb[:, mb * SEG:(mb + 1) * SEG],
                            func=Act.Square),
                            reads=(buf(fm_b),), writes=(buf("sqbuf"),),
                            label=f"ssq_{name}_{seg}_{mb}")
                        S.op("pe", lambda t_, mb=mb: t_.matmul(
                            out=ssq_ps, lhsT=sel_sb[:, mb * nsel:(mb + 1) * nsel],
                            rhs=sqbuf[:, 0:512], start=(mb == 0), stop=(mb == nmb - 1)),
                            reads=(buf("sqbuf"), buf(sel_b)), writes=(buf(ssq_psb),),
                            label=f"ssqmm_{name}_{seg}_{mb}")
                        yield

            def bg_gen():
                for cb in range(8):
                    S.op("pe", lambda t_, cb=cb: t_.matmul(
                        out=psG[0:8, 0:512], lhsT=wbgT[:, cb * 8:(cb + 1) * 8],
                        rhs=h2f[:, cb * 512:(cb + 1) * 512], start=(cb == 0), stop=(cb == 7)),
                        reads=(buf("wbgT"), buf("h2f")), writes=(buf("psG"),),
                        label=f"proj_bg_{seg}_{cb}")
                S.op("dve", lambda v: v.tensor_scalar_add(
                    out=bgrows, in0=psG[0:8, 0:512], scalar1=bbg_col[:, :]),
                    reads=(buf("psG"), buf("bbg")), writes=(buf("srows"),),
                    label=f"bgev_{seg}")
                yield

            def rows_gen():
                yield
                S.op("act", lambda a: a.activation(
                    out=ab8, in_=bgrows, func=Act.Abs),
                    reads=(buf("srows"),), writes=(buf("srows"),), label=f"sp_abs_{seg}")
                S.op("act", lambda a: a.activation(
                    out=ab8, in_=ab8, func=Act.Exp, scale=cpool[0:8, 5:6]),
                    reads=(buf("srows"), buf("cpool")), writes=(buf("srows"),),
                    label=f"sp_exp_{seg}")
                S.op("act", lambda a: a.activation(
                    out=ab8, in_=ab8, func=Act.Ln, bias=cpool[0:8, 2:3]),
                    reads=(buf("srows"), buf("cpool")), writes=(buf("srows"),),
                    label=f"sp_ln_{seg}")
                yield
                S.op("dve", lambda v: v.scalar_tensor_tensor(
                    out=srows[32:36, 0:SEG], in0=srows[0:4, 0:SEG], scalar=0.0,
                    in1=srows[0:4, SEG:2 * SEG], op0=Alu.max, op1=Alu.add),
                    reads=(buf("srows"),), writes=(buf("srows"),),
                    label=f"sp_gk_{seg}")
                S.op("dve", lambda v: v.tensor_scalar_mul(
                    out=srows[32:36, SEG:2 * SEG], in0=srows[32:36, 0:SEG],
                    scalar1=negA[32:36, :]),
                    reads=(buf("srows"), buf("negA")), writes=(buf("srows"),),
                    label=f"gkv_{seg}")
                yield
                for ch in range(NT):
                    S.op("dve", lambda v, ch=ch: v.tensor_tensor_scan(
                        out=srcrows[0:4, ch * 128:(ch + 1) * 128], data0=fones[32:36, 0:128],
                        data1=srows[32:36, SEG + ch * 128:SEG + (ch + 1) * 128], initial=0.0,
                        op0=Alu.mult, op1=Alu.add),
                        reads=(buf("srows"), buf("fones")), writes=(buf("srcrows"),),
                        label=f"gscan_{seg}_{ch}")
                yield
                S.op("dve", lambda v: v.scalar_tensor_tensor(
                    out=srcrows[64:72, :], in0=srows[0:8, 0:SEG], scalar=0.0,
                    in1=srows[0:8, SEG:2 * SEG], op0=Alu.min, op1=Alu.subtract),
                    reads=(buf("srows"),), writes=(buf("srcrows"),),
                    label=f"lnB_{seg}")
                yield

            run_pair([
                proj_gen("q", qraw, "qraw", qfm, "qfm", wqT, "wqT", 8, C, 0,
                         selq, "selq", 16, psH[0:16, 0:512], "psH", None),
                proj_gen("k", kraw, "kraw", kfm, "kfm", wkT, "wkT", 4, 512, 32,
                         selkd, "selkd", 4, psF[0:4, 0:512], "psF", None),
                proj_gen("v", vraw, "vraw", vfm, "vfm", wvT, "wvT", 2, 256, 48,
                         None, None, 0, None, None, None),
                bg_gen(),
                rows_gen(),
            ])
            S.op("act", lambda a: a.activation(
                out=srcrows[96:112, :], in_=psH[0:16, 0:512], func=Act.Ln,
                bias=cpool[0:16, 1:2]),
                reads=(buf("psH"), buf("cpool")), writes=(buf("srcrows"),),
                label=f"lnQ_{seg}")
            S.op("act", lambda a: a.activation(
                out=srcrows[32:36, :], in_=psF[0:4, 0:512], func=Act.Ln,
                bias=cpool[0:4, 1:2]),
                reads=(buf("psF"), buf("cpool")), writes=(buf("srcrows"),),
                label=f"lnK_{seg}")

            S.op("pe", lambda t_: t_.matmul(
                out=psH[0:68, 0:512], lhsT=combC[:, :], rhs=srcrows[:, :],
                start=True, stop=True),
                reads=(buf("combC"), buf("srcrows")), writes=(buf("psH"),),
                label=f"comb_{seg}")
            S.op("dve", lambda v: v.tensor_copy(out=rows2[:, :], in_=psH[0:68, 0:512]),
                 reads=(buf("psH"),), writes=(buf("rows2"),), label=f"r2f_{seg}")
            S.op("act", lambda a: a.activation(out=rows2h[:, :], in_=psH[0:28, 0:512],
                                               func=Act.Copy),
                 reads=(buf("psH"),), writes=(buf("rows2h"),), label=f"r2h_{seg}")
            S.op("dve", lambda v: v.tensor_tensor(
                out=rows2l[:, :], in0=rows2[0:28, :], in1=rows2h[:, :], op=Alu.subtract),
                reads=(buf("rows2"), buf("rows2h")), writes=(buf("rows2l"),),
                label=f"r2l_{seg}")

            if stage == 2:
                S.dma("gp", lambda g, seg=seg: g.dma_start(
                    out=dbg_q.rearrange("(s p) w -> s p w", p=128)[seg], in_=qfm[:, :]),
                    reads=(buf("qfm"),), label=f"dbgq_{seg}")
                S.dma("gp", lambda g, seg=seg: g.dma_start(
                    out=dbg_k.rearrange("(s p) w -> s p w", p=128)[seg], in_=kfm[:, :]),
                    reads=(buf("kfm"),), label=f"dbgk_{seg}")
                S.dma("gp", lambda g, seg=seg: g.dma_start(
                    out=dbg_v.rearrange("(s p) w -> s p w", p=128)[seg], in_=vfm[:, :]),
                    reads=(buf("vfm"),), label=f"dbgv_{seg}")
                S.dma("gp", lambda g, seg=seg: g.dma_start(
                    out=dbg_r.rearrange("(s p) w -> s p w", p=68)[seg], in_=rows2[:, :]),
                    reads=(buf("rows2"),), label=f"dbgr_{seg}")
                S.dma("gp", lambda g, seg=seg: g.dma_start(
                    out=dbg_s.rearrange("(s p) w -> s p w", p=128)[seg], in_=sgate[:, :]),
                    reads=(buf("qfm"),), label=f"dbgs_{seg}")
                S.dma("gp", lambda g, seg=seg: g.dma_start(
                    out=dst_x[seg], in_=xseg[seg % 2][:, :].rearrange("p (t c) -> p t c", t=NT)),
                    reads=(xb,), label=f"xstore_{seg}")
                continue

            def gate_gen(ct):
                gp_ = ct % 2
                sg = sgate_[gp_]
                bsg = f"sgate{gp_}"
                for nh in range(2):
                    for cb in range(8):
                        S.op("pe", lambda t_, cb=cb, nh=nh, ct=ct: t_.matmul(
                            out=psF[:, 0:512], lhsT=h2f[:, cb * 512 + ct * 128: cb * 512 + (ct + 1) * 128],
                            rhs=wgT[:, cb * 1024 + nh * 512: cb * 1024 + (nh + 1) * 512],
                            start=(cb == 0), stop=(cb == 7)),
                            reads=(buf("h2f"), buf("wgT")), writes=(buf("psF"),),
                            label=f"gproj_{seg}_{ct}_{nh}_{cb}")
                    S.op("act", lambda a, nh=nh: a.activation(
                        out=ebuf[:, nh * 512:(nh + 1) * 512], in_=psF[:, 0:512],
                        func=Act.Exp, scale=cpool[:, 5:6]),
                        reads=(buf("psF"), buf("cpool")), writes=(buf("ebuf"),),
                        label=f"gexp_{seg}_{ct}_{nh}")
                    S.op("act", lambda a, nh=nh: a.activation(
                        out=gbuf[:, nh * 512:(nh + 1) * 512], in_=psF[:, 0:512],
                        func=Act.Copy),
                        reads=(buf("psF"),), writes=(buf("gbuf"),),
                        label=f"gcopy_{seg}_{ct}_{nh}")
                    yield
                S.op("act", lambda a: a.activation(
                    out=ebuf[:, :], in_=ebuf[:, :], func=Act.Ln, bias=cpool[:, 2:3]),
                    reads=(buf("ebuf"), buf("cpool")), writes=(buf("ebuf"),),
                    label=f"gln_{seg}_{ct}")
                yield
                S.op("act", lambda a: a.activation(
                    out=ebuf[:, :], in_=ebuf[:, :], func=Act.Exp, scale=cpool[:, 5:6]),
                    reads=(buf("ebuf"), buf("cpool")), writes=(buf("ebuf"),),
                    label=f"gsig_{seg}_{ct}")
                yield
                S.op("dve", lambda v, sg=sg: v.tensor_tensor(
                    out=sg[:, :], in0=gbuf[:, :], in1=ebuf[:, :], op=Alu.mult),
                    reads=(buf("gbuf"), buf("ebuf")), writes=(buf(bsg),),
                    label=f"gsilu_{seg}_{ct}")
                yield

            def tail_gen(c, seg=seg, xb=xb):
                    S.op("act", lambda a: a.activation(out=sqbuf[:, :], in_=obuf[:, :],
                                                               func=Act.Square),
                         reads=(buf("obuf"),), writes=(buf("sqbuf"),), label=f"osq_{seg}_{c}")
                    S.op("dve", lambda v: v.tensor_reduce(
                        out=oinv[:, 0:16], in_=sqbuf[:, :].rearrange("p (h e) -> p h e", h=16),
                        axis=Ax.X, op=Alu.add),
                        reads=(buf("sqbuf"),), writes=(buf("oinv"),), label=f"ored_{seg}_{c}")
                    yield
                    S.op("act", lambda a: a.activation(
                        out=oinv[:, 0:16], in_=oinv[:, 0:16], func=Act.Ln,
                        scale=cpool[:, 3:4], bias=cpool[:, 0:1]),
                        reads=(buf("oinv"), buf("cpool")), writes=(buf("oinv"),),
                        label=f"oln_{seg}_{c}")
                    yield
                    S.op("act", lambda a: a.activation(
                        out=oinv[:, 0:16], in_=oinv[:, 0:16], func=Act.Exp,
                        scale=cpool[:, 4:5], bias=cpool[:, 7:8]),
                        reads=(buf("oinv"), buf("cpool")), writes=(buf("oinv"),),
                        label=f"oexp_{seg}_{c}")
                    yield
                    S.op("dve", lambda v, c=c: v.tensor_tensor(
                        out=ogbuf[:, :], in0=obuf[:, :], in1=sgate_[c % 2][:, :],
                        op=Alu.mult),
                        reads=(buf("obuf"), buf(f"sgate{c % 2}")),
                        writes=(buf("ogbuf"),), label=f"og1_{seg}_{c}")
                    yield
                    S.op("dve", lambda v: v.tensor_tensor(
                        out=ogbuf[:, :].rearrange("p (h w) -> p h w", w=64),
                        in0=ogbuf[:, :].rearrange("p (h w) -> p h w", w=64),
                        in1=oinv[:, 0:16].rearrange("p h -> p h ()").broadcast_to([128, 16, 64]),
                        op=Alu.mult),
                        reads=(buf("ogbuf"), buf("oinv")),
                        writes=(buf("ogbuf"),), label=f"og2_{seg}_{c}")
                    yield
                    transpose_src(ogbuf, buf("ogbuf"),
                                  lambda half: ogf[:, half * 512:(half + 1) * 512],
                                  buf("hfr0"), f"og_{seg}_{c}")
                    xt = tile_x(seg, c)
                    for nh in range(2):
                        for cb in range(8):
                            S.op("pe", lambda t_, cb=cb, nh=nh: t_.matmul(
                                out=psF[:, 0:512], lhsT=ogf[:, cb * 128:(cb + 1) * 128],
                                rhs=wpT[:, cb * 1024 + nh * 512: cb * 1024 + (nh + 1) * 512],
                                start=(cb == 0), stop=(cb == 7)),
                                reads=(buf("hfr0"), buf("wpT")), writes=(buf("psF"),),
                                label=f"wp_{seg}_{c}_{nh}_{cb}")
                        S.op("dve", lambda v, nh=nh, xt=xt: v.tensor_tensor(
                            out=xt[:, nh * 512:(nh + 1) * 512],
                            in0=xt[:, nh * 512:(nh + 1) * 512], in1=psF[:, 0:512], op=Alu.add),
                            reads=(buf("psF"), xb), writes=(xb,), label=f"wpev_{seg}_{c}_{nh}")
                        yield



            def cpre_gen(c):
                chs = slice(c * 128, (c + 1) * 128)
                ce = (c + 1) * 128
                cp_ = c % 2
                colpk = colpk_[cp_]; gLbc = gLbc_[cp_]
                gLrow = gLrow_[cp_]; epack = epack_[cp_]
                bcolpk = f"colpk{cp_}"; bgLbc = f"gLbc{cp_}"
                bgLrow = f"gLrow{cp_}"; bepack = "epack"
                S.op("pe", lambda t_, c=c, ce=ce: t_.matmul(
                    out=psH[0:1, 0:4], lhsT=srcrows[0:4, ce - 1:ce], rhs=idenF4[:, :],
                    start=True, stop=True),
                    reads=(buf("srcrows"), buf("idenF4")), writes=(buf("psH"),),
                    label=f"gLtr_{seg}_{c}")
                S.op("act", lambda a, gLrow=gLrow: a.activation(out=gLrow[:, :], in_=psH[0:1, 0:4],
                                                   func=Act.Exp),
                     reads=(buf("psH"),), writes=(buf(bgLrow),), label=f"gLexp_{seg}_{c}")
                S.op("pe", lambda t_, gLrow=gLrow: t_.matmul(
                    out=psH[:, 16:20], lhsT=ones1[0:1, 0:128], rhs=gLrow[:, :],
                    start=True, stop=True),
                    reads=(buf(bgLrow), buf("ones1x512")), writes=(buf("psH"),),
                    label=f"gLbc_{seg}_{c}")
                S.op("dve", lambda v, gLbc=gLbc: v.tensor_copy(out=gLbc[:, :], in_=psH[:, 16:20]),
                     reads=(buf("psH"),), writes=(buf(bgLbc),), label=f"gLev_{seg}_{c}")
                yield
                S.op("dve", lambda v, chs=chs, epack=epack: v.tensor_copy(
                    out=epack[0:28, :], in_=rows2[0:28, chs]),
                    reads=(buf("rows2"),), writes=(buf(bepack),), label=f"ep1_{seg}_{c}")
                S.op("dve", lambda v, ce=ce, epack=epack: v.tensor_scalar_add(
                    out=epack[0:4, :], in0=epack[0:4, :],
                    scalar1=srcrows[0:4, ce - 1:ce]),
                    reads=(buf(bepack), buf("srcrows")), writes=(buf(bepack),),
                    label=f"ep2_{seg}_{c}")
                S.op("dve", lambda v, chs=chs, epack=epack: v.tensor_copy(
                    out=epack[64:68, :], in_=rows2[64:68, chs]),
                    reads=(buf("rows2"),), writes=(buf(bepack),), label=f"ep3_{seg}_{c}")
                S.op("act", lambda a, epack=epack: a.activation(out=epack[0:28, :], in_=epack[0:28, :],
                                                   func=Act.Exp),
                     reads=(buf(bepack),), writes=(buf(bepack),), label=f"ep4_{seg}_{c}")
                yield
                S.op("pe", lambda t_, epack=epack: t_.matmul(
                    out=psH[:, 32:64], lhsT=epack[:, :], rhs=trsel[:, :],
                    start=True, stop=True),
                    reads=(buf(bepack), buf("trsel")), writes=(buf("psH"),),
                    label=f"eptr_{seg}_{c}")
                S.op("dve", lambda v, colpk=colpk: v.tensor_copy(out=colpk[:, :], in_=psH[:, 32:64]),
                     reads=(buf("psH"),), writes=(buf(bcolpk),), label=f"cpev_{seg}_{c}")
                yield


            qfm_r = qfm[:, :].rearrange("p (b t w) -> p b t w", t=NT, w=128)

            def pre_gen(hk, c):
                chs = slice(c * 128, (c + 1) * 128)
                cp_ = c % 2
                colpk = colpk_[cp_]; gLbc = gLbc_[cp_]
                bcolpk = f"colpk{cp_}"; bgLbc = f"gLbc{cp_}"
                p = hk % 2
                Mtb = Mtb_[p]; Atb = Atb_[p]; decb = decb_[p]
                Mpow = Mpow_[p]; Ttb = Ttb_[p]; P0 = P0_[p]
                Pc = Pc_[p]; Ub = Ub_[p]; Wb = Wb_[p]; Rw = Rw_[p]
                pow_ps = psC if p == 0 else psG
                pow_b = "psC" if p == 0 else "psG"
                bMtb = f"Mtb{p}"; bAtb = f"Atb{p}"; bdecb = "decb"
                bMpow = f"Mpow{p}"; bTtb = f"Ttb{p}"; bP0 = f"P0{p}"
                bUb = f"Ub{p}"; bWb = f"Wb{p}"; bRw = f"Rw{p}"
                kcols = slice(hk * SEG + c * 128, hk * SEG + (c + 1) * 128)
                kslA = kfm[0:64, kcols]
                kslB = kfm[64:128, kcols]
                order = [4 * hk, 4 * hk + 2, 4 * hk + 1, 4 * hk + 3]
                S.op("pe", lambda t_, kslA=kslA: t_.matmul(
                    out=psB[:, 128:256], lhsT=kslA, rhs=kslA, start=True, stop=True,
                    tile_position=(0, 0)),
                    reads=(buf("kfm"),), writes=(buf("psB"),),
                    label=f"gKK_{seg}_{c}_{hk}")
                for hl, rsrc, rb in ((0, rows2h, "rows2h"), (1, rows2l, "rows2l")):
                    S.op("pe", lambda t_, rsrc=rsrc, hl=hl: t_.matmul(
                        out=psB[:, 0:128], lhsT=selrow[:, hk * 128:(hk + 1) * 128],
                        rhs=rsrc[:, chs], start=(hl == 0), stop=(hl == 1)),
                        reads=(buf("selrow"), buf(rb)), writes=(buf("psB"),),
                        label=f"zm_{seg}_{c}_{hk}_{hl}")
                S.op("dve", lambda v: v.scalar_tensor_tensor(
                    out=psB[:, 0:128], in0=psB[:, 0:128],
                    scalar=colpk[:, 28 + hk:29 + hk], in1=maskM[:, :],
                    op0=Alu.subtract, op1=Alu.add),
                    reads=(buf("psB"), buf(bcolpk), buf("maskM")),
                    writes=(buf("psB"),), label=f"zms_{seg}_{c}_{hk}")
                S.op("act", lambda a: a.activation(out=decb[:, 0:128], in_=psB[:, 0:128],
                                                   func=Act.Exp),
                     reads=(buf("psB"),), writes=(buf(bdecb),),
                     label=f"zme_{seg}_{c}_{hk}")
                S.op("dve", lambda v: v.tensor_tensor(
                    out=Mtb[:, :], in0=psB[:, 128:256], in1=decb[:, 0:128], op=Alu.mult),
                    reads=(buf("psB"), buf(bdecb)), writes=(buf(bMtb),),
                    label=f"mt_{seg}_{c}_{hk}")
                yield
                yield
                S.op("pe", lambda t_: t_.matmul(
                    out=pow_ps[:, 0:128], lhsT=Mtb[:, :], rhs=iden[:, :],
                    start=True, stop=True),
                    reads=(buf(bMtb), buf("iden128")), writes=(buf(pow_b),),
                    label=f"m1_{seg}_{c}_{hk}")
                yield
                S.op("dve", lambda v: v.tensor_copy(
                    out=Mpow[:, 0:128], in_=pow_ps[:, 0:128]),
                    reads=(buf(pow_b),), writes=(buf(bMpow),),
                    label=f"m1ev_{seg}_{c}_{hk}")
                yield
                prevM, prevMt = 0, None
                slot = 1
                for lev in range(5):
                    mtprev = Mtb[:, :] if prevMt is None else Mpow[:, prevMt * 128:(prevMt + 1) * 128]
                    mprev = Mpow[:, prevM * 128:(prevM + 1) * 128]
                    S.op("pe", lambda t_, mtprev=mtprev, mprev=mprev: t_.matmul(
                        out=pow_ps[:, 0:128], lhsT=mtprev, rhs=mprev,
                        start=True, stop=True),
                        reads=(buf(bMpow), buf(bMtb)), writes=(buf(pow_b),),
                        label=f"sqm_{seg}_{c}_{hk}_{lev}")
                    S.op("pe", lambda t_, mtprev=mtprev, mprev=mprev: t_.matmul(
                        out=pow_ps[:, 128:256], lhsT=mprev, rhs=mtprev,
                        start=True, stop=True),
                        reads=(buf(bMpow), buf(bMtb)), writes=(buf(pow_b),),
                        label=f"sqt_{seg}_{c}_{hk}_{lev}")
                    yield
                    S.op("act", lambda a, slot=slot: a.activation(
                        out=Mpow[:, slot * 128:(slot + 2) * 128], in_=pow_ps[:, 0:256],
                        func=Act.Copy),
                        reads=(buf(pow_b),), writes=(buf(bMpow),),
                        label=f"sqev_{seg}_{c}_{hk}_{lev}")
                    yield
                    prevM, prevMt = slot, slot + 1
                    slot += 2
                S.op("pe", lambda t_, prevM=prevM, prevMt=prevMt: t_.matmul(
                    out=pow_ps[:, 0:128], lhsT=Mpow[:, prevMt * 128:(prevMt + 1) * 128],
                    rhs=Mpow[:, prevM * 128:(prevM + 1) * 128], start=True, stop=True),
                    reads=(buf(bMpow),), writes=(buf(pow_b),),
                    label=f"m64_{seg}_{c}_{hk}")
                yield
                S.op("act", lambda a: a.activation(
                    out=Mpow[:, 0:128], in_=pow_ps[:, 0:128], func=Act.Copy),
                    reads=(buf(pow_b),), writes=(buf(bMpow),),
                    label=f"m64ev_{seg}_{c}_{hk}")
                yield
                S.op("dve", lambda v: v.scalar_tensor_tensor(
                    out=P0[:, :], in0=Mtb[:, :], scalar=-1.0, in1=iden[:, :],
                    op0=Alu.mult, op1=Alu.add),
                    reads=(buf(bMtb), buf("iden128")), writes=(buf(bP0),),
                    label=f"p0_{seg}_{c}_{hk}")
                yield
                mslots = {2: 1, 4: 3, 8: 5, 16: 7, 32: 9, 64: 0}
                cur = P0
                curb = bP0
                for pi, pw in enumerate([2, 4, 8, 16, 32, 64]):
                    S.op("pe", lambda t_, pw=pw, cur=cur: t_.matmul(
                        out=pow_ps[:, 256:384], lhsT=Mpow[:, mslots[pw] * 128:(mslots[pw] + 1) * 128],
                        rhs=cur[:, 0:128], start=True, stop=True),
                        reads=(buf(bMpow), buf(curb)), writes=(buf(pow_b),),
                        label=f"tt_{seg}_{c}_{hk}_{pw}")
                    yield
                    if pw == 64:
                        nxt, nxtb = Ttb, bTtb
                    else:
                        nxt, nxtb = Pc[pi % 2], f"Pc{p}_{pi % 2}"
                    S.op("dve", lambda v, nxt=nxt, cur=cur: v.tensor_tensor(
                        out=nxt[:, 0:128], in0=cur[:, 0:128], in1=pow_ps[:, 256:384],
                        op=Alu.add),
                        reads=(buf(pow_b), buf(curb)), writes=(buf(nxtb),),
                        label=f"ttev_{seg}_{c}_{hk}_{pw}")
                    yield
                    cur, curb = nxt, nxtb
                S.op("pe", lambda t_, kslA=kslA: t_.matmul(
                    out=psA[:, 0:256], lhsT=kslA,
                    rhs=qfm_r[0:64, 2 * hk:2 * hk + 2, c, :],
                    start=True, stop=True, tile_position=(0, 0)),
                    reads=(buf("kfm"), buf("qfm")), writes=(buf("psA"),),
                    label=f"gKQa_{seg}_{c}_{hk}")
                S.op("pe", lambda t_, kslB=kslB: t_.matmul(
                    out=psD[:, 128:384], lhsT=kslB,
                    rhs=qfm_r[64:128, 2 * hk:2 * hk + 2, c, :],
                    start=True, stop=True, tile_position=(64, 0)),
                    reads=(buf("kfm"), buf("qfm")), writes=(buf("psD"),),
                    label=f"gKQb_{seg}_{c}_{hk}")
                for s in range(4):
                    h = order[s]
                    for hl, rsrc, rb in ((0, rows2h, "rows2h"), (1, rows2l, "rows2l")):
                        S.op("pe", lambda t_, s=s, h=h, rsrc=rsrc, hl=hl: t_.matmul(
                            out=psB[:, s * 128:(s + 1) * 128],
                            lhsT=selrow[:, (4 + h) * 128:(5 + h) * 128],
                            rhs=rsrc[:, chs], start=(hl == 0), stop=(hl == 1)),
                            reads=(buf("selrow"), buf(rb)), writes=(buf("psB"),),
                            label=f"z4_{seg}_{c}_{hk}_{s}_{hl}")
                S.op("dve", lambda v: v.scalar_tensor_tensor(
                    out=psB[:, 0:512], in0=psB[:, 0:512],
                    scalar=colpk[:, 28 + hk:29 + hk], in1=maskA[:, :],
                    op0=Alu.subtract, op1=Alu.add),
                    reads=(buf("psB"), buf(bcolpk), buf("maskA")),
                    writes=(buf("psB"),), label=f"z4s_{seg}_{c}_{hk}")
                S.op("act", lambda a: a.activation(out=decb[:, :], in_=psB[:, 0:512],
                                                   func=Act.Exp),
                     reads=(buf("psB"),), writes=(buf(bdecb),),
                     label=f"z4e_{seg}_{c}_{hk}")
                S.op("dve", lambda v: v.tensor_tensor(
                    out=Atb[:, 0:256], in0=psA[:, 0:256], in1=decb[:, 0:256], op=Alu.mult),
                    reads=(buf("psA"), buf(bdecb)), writes=(buf(bAtb),),
                    label=f"at_{seg}_{c}_{hk}")
                S.op("dve", lambda v: v.tensor_tensor(
                    out=Atb[:, 256:512], in0=psD[:, 128:384], in1=decb[:, 256:512], op=Alu.mult),
                    reads=(buf("psD"), buf(bdecb)), writes=(buf(bAtb),),
                    label=f"at2_{seg}_{c}_{hk}")
                yield

            def seq_gen(hk, c):
                chs = slice(c * 128, (c + 1) * 128)
                cp_ = c % 2
                colpk = colpk_[cp_]; gLbc = gLbc_[cp_]
                bcolpk = f"colpk{cp_}"; bgLbc = f"gLbc{cp_}"
                p = hk % 2
                Mtb = Mtb_[p]; Atb = Atb_[p]; decb = decb_[p]
                Mpow = Mpow_[p]; Ttb = Ttb_[p]; P0 = P0_[p]
                Pc = Pc_[p]; Ub = Ub_[p]; Wb = Wb_[p]; Rw = Rw_[p]
                pow_ps = psC if p == 0 else psG
                pow_b = "psC" if p == 0 else "psG"
                bMtb = f"Mtb{p}"; bAtb = f"Atb{p}"; bdecb = "decb"
                bMpow = f"Mpow{p}"; bTtb = f"Ttb{p}"; bP0 = f"P0{p}"
                bUb = f"Ub{p}"; bWb = f"Wb{p}"; bRw = f"Rw{p}"
                kcols = slice(hk * SEG + c * 128, hk * SEG + (c + 1) * 128)
                kslA = kfm[0:64, kcols]
                kslB = kfm[64:128, kcols]
                order = [4 * hk, 4 * hk + 2, 4 * hk + 1, 4 * hk + 3]
                S.op("pe", lambda t_, kslA=kslA: t_.matmul(
                    out=psD[:, 256:320], lhsT=kslA,
                    rhs=Sbf[0:64, hk * 64:(hk + 1) * 64], start=True, stop=True,
                    tile_position=(0, 0)),
                    reads=(buf("kfm"), buf("Sbf")), writes=(buf("psD"),),
                    label=f"rmm_{seg}_{c}_{hk}")
                S.op("dve", lambda v: v.tensor_scalar_mul(
                    out=Rw[:, :], in0=psD[:, 256:320], scalar1=colpk[:, 4 + hk:5 + hk]),
                    reads=(buf("psD"), buf(bcolpk)), writes=(buf(bRw),),
                    label=f"rw_{seg}_{c}_{hk}")
                yield
                vb = hk // 2
                vbase = (hk % 2) * 64
                S.op("pe", lambda t_, vb=vb, vbase=vbase: t_.matmul(
                    out=psD[:, 320:384],
                    lhsT=vfm[vbase:vbase + 64, vb * SEG + c * 128: vb * SEG + (c + 1) * 128],
                    rhs=iden[vbase:vbase + 64, vbase:vbase + 64],
                    start=True, stop=True, tile_position=(vbase, 0)),
                    reads=(buf("vfm"), buf("iden128")), writes=(buf("psD"),),
                    label=f"vtm_{seg}_{c}_{hk}")
                S.op("dve", lambda v: v.scalar_tensor_tensor(
                    out=Wb[:, :], in0=psD[:, 320:384], scalar=colpk[:, 24 + hk:25 + hk],
                    in1=Rw[:, :], op0=Alu.mult, op1=Alu.subtract),
                    reads=(buf("psD"), buf(bcolpk), buf(bRw)), writes=(buf(bWb),),
                    label=f"wb_{seg}_{c}_{hk}")
                yield
                S.op("pe", lambda t_: t_.matmul(
                    out=psD[:, 0:64], lhsT=Ttb[:, :], rhs=Wb[:, :],
                    start=True, stop=True),
                    reads=(buf(bTtb), buf(bWb)), writes=(buf("psD"),),
                    label=f"umm_{seg}_{c}_{hk}")
                S.op("act", lambda a: a.activation(out=Ub[:, :], in_=psD[:, 0:64],
                                                   func=Act.Copy),
                     reads=(buf("psD"),), writes=(buf(bUb),),
                     label=f"uev_{seg}_{c}_{hk}")
                yield
                for s in range(4):
                    S.op("pe", lambda t_, s=s: t_.matmul(
                        out=psE[:, s * 64:(s + 1) * 64], lhsT=Atb[:, s * 128:(s + 1) * 128],
                        rhs=Ub[:, :], start=True, stop=True),
                        reads=(buf(bAtb), buf(bUb)), writes=(buf("psE"),),
                        label=f"oi_{seg}_{c}_{hk}_{s}")
                for s in range(4):
                    h = order[s]
                    qb_ = h // 2
                    qbase = (h % 2) * 64
                    outap = (psE[:, 256 + s * 64:256 + (s + 1) * 64] if qbase == 0
                             else psD[:, 128 + (s - 2) * 64:128 + (s - 1) * 64])
                    outb = buf("psE") if qbase == 0 else buf("psD")
                    S.op("pe", lambda t_, s=s, qb_=qb_, qbase=qbase, outap=outap: t_.matmul(
                        out=outap,
                        lhsT=qfm[qbase:qbase + 64, qb_ * SEG + c * 128: qb_ * SEG + (c + 1) * 128],
                        rhs=Sbf[qbase:qbase + 64, hk * 64:(hk + 1) * 64],
                        start=True, stop=True, tile_position=(qbase, 0)),
                        reads=(buf("qfm"), buf("Sbf")), writes=(outb,),
                        label=f"on_{seg}_{c}_{hk}_{s}")
                S.op("act", lambda a: a.activation(out=ointer[:, 0:128], in_=psE[:, 256:384],
                                                   func=Act.Copy),
                     reads=(buf("psE"),), writes=(buf("ointer"),),
                     label=f"onev_{seg}_{c}_{hk}")
                S.op("act", lambda a: a.activation(out=ointer[:, 128:256], in_=psD[:, 128:256],
                                                   func=Act.Copy),
                     reads=(buf("psD"),), writes=(buf("ointer"),),
                     label=f"onev2_{seg}_{c}_{hk}")
                for s in range(4):
                    h = order[s]
                    S.op("dve", lambda v, s=s, h=h: v.scalar_tensor_tensor(
                        out=obuf[:, h * 64:(h + 1) * 64], in0=ointer[:, s * 64:(s + 1) * 64],
                        scalar=colpk[:, 8 + h:9 + h], in1=psE[:, s * 64:(s + 1) * 64],
                        op0=Alu.mult, op1=Alu.add),
                        reads=(buf("ointer"), buf(bcolpk), buf("psE")),
                        writes=(buf("obuf"),), label=f"oc_{seg}_{c}_{hk}_{s}")
                yield
                S.op("pe", lambda t_, kslA=kslA: t_.matmul(
                    out=psD[:, 384:512], lhsT=kslA, rhs=iden64x2[:, :],
                    start=True, stop=True, tile_position=(0, 0)),
                    reads=(buf("kfm"), buf("iden64x2")), writes=(buf("psD"),),
                    label=f"ktm_{seg}_{c}_{hk}")
                S.op("dve", lambda v: v.tensor_scalar_mul(
                    out=ktm2[:, :], in0=psD[:, 384:512], scalar1=colpk[:, 0 + hk:1 + hk]),
                    reads=(buf("psD"), buf(bcolpk)), writes=(buf("ktm2"),),
                    label=f"khat_{seg}_{c}_{hk}")
                S.op("pe", lambda t_: t_.matmul(
                    out=psD[:, 64:128], lhsT=ktm2[:, :], rhs=Ub[:, :],
                    start=True, stop=True),
                    reads=(buf("ktm2"), buf(bUb)), writes=(buf("psD"),),
                    label=f"smm_{seg}_{c}_{hk}")
                S.op("dve", lambda v: v.scalar_tensor_tensor(
                    out=Sf32[:, hk * 64:(hk + 1) * 64], in0=Sf32[:, hk * 64:(hk + 1) * 64],
                    scalar=gLbc[:, hk:hk + 1], in1=psD[:, 64:128],
                    op0=Alu.mult, op1=Alu.add),
                    reads=(buf("Sf32"), buf(bgLbc), buf("psD")),
                    writes=(buf("Sf32"),), label=f"s1_{seg}_{c}_{hk}")
                yield
                S.op("gpc", lambda g: g.tensor_copy(
                    out=Sbf[:, hk * 64:(hk + 1) * 64], in_=Sf32[:, hk * 64:(hk + 1) * 64]),
                    reads=(buf("Sf32"),), writes=(buf("Sbf"),),
                    label=f"sbf_{seg}_{c}_{hk}")

            run_pair([cpre_gen(0)])
            run_pair([pre_gen(0, 0), pre_gen(1, 0), gate_gen(0)])
            for c in range(NT):
                g1 = [seq_gen(0, c), seq_gen(1, c), pre_gen(2, c), pre_gen(3, c)]
                if c > 0:
                    g1.append(ffn_gen(seg, c - 1, xb, "f2", sps=psH, spsb="psH", single=True))
                run_pair(g1)
                if c + 1 < NT:
                    run_pair([seq_gen(2, c), seq_gen(3, c), gate_gen(c + 1),
                              cpre_gen(c + 1)])
                    run_pair([tail_gen(c), pre_gen(0, c + 1), pre_gen(1, c + 1)])
                else:
                    run_pair([seq_gen(2, c), seq_gen(3, c)])
                    if seg + 1 < NSEG:
                        xbn = xbufs[(seg + 1) % 2]
                        S.dma("sp", lambda s_, seg=seg: s_.dma_start(
                            out=xseg[(seg + 1) % 2][:, :].rearrange("p (t c) -> p t c", t=NT),
                            in_=src_x[seg + 1]),
                            writes=(xbn,), label=f"xload_{seg + 1}")
                        run_pair([tail_gen(c), ffn_gen(seg + 1, 1, xbn, "f1")])
                    else:
                        run_pair([tail_gen(c)])
            g4 = [ffn_gen(seg, NT - 1, xb, "f2", sps=psH, spsb="psH", single=True)]
            if seg + 1 < NSEG:
                g4 += [ffn_gen(seg + 1, 0, xbn, "f1")]
            run_pair(g4)
            if seg + 1 < NSEG:
                run_pair([ffn_gen(seg + 1, 2, xbn, "f1"),
                          ffn_gen(seg + 1, 3, xbn, "f1")])
            S.dma("gp", lambda g, seg=seg: g.dma_start(
                out=dst_x[seg], in_=xseg[seg % 2][:, :].rearrange("p (t c) -> p t c", t=NT)),
                reads=(xb,), label=f"xstore_{seg}")
            continue

        order = S.finalize()
        streams, totals, run_stream = S.emit(nc, sems, None)
        with nc.Block() as block:
            @block.tensor
            def _(e):
                run_stream("pe", e)

            @block.vector
            def _(e):
                run_stream("dve", e)

            @block.scalar
            def _(e):
                run_stream("act", e)

            @block.sync
            def _(e):
                run_stream("sp", e)

            @block.gpsimd
            def _(e):
                run_stream(("gp", "gpc"), e)
                if totals["gp"]:
                    e.wait_ge(sems["gp"], totals["gp"])
                if totals["gpc"]:
                    e.wait_ge(sems["gpc"], totals["gpc"])
    return nc


def kernel(**inputs):
    T = np.asarray(inputs["L0_attn_x"]).shape[1]
    nc = build_program(T=T, stage=4)
    in_maps = [host_layout(inputs, core) for core in range(8)]
    res = run_bass_kernel_spmd(nc, in_maps, core_ids=list(range(8)))
    return np.stack([r["out"] for r in res.results]).astype(np.float32)


# version 17
# speedup vs baseline: 1.1155x; 1.0044x over previous
import math
import numpy as np
import ml_dtypes
import concourse.bass as bass
import concourse.mybir as mybir
from concourse.bass_utils import run_bass_kernel_spmd

F32 = mybir.dt.float32
BF16 = mybir.dt.bfloat16
Alu = mybir.AluOpType
Act = mybir.ActivationFunctionType
Ax = mybir.AxisListType

H, HKV, D, CONVK = 16, 4, 64, 4
C = 1024
L = 128
SEG = 512
NEG = -1e30

ENGINES = ("pe", "dve", "act", "sp", "gp", "gpc")


class Buf:
    __slots__ = ("name", "last_write", "readers")

    def __init__(self, name):
        self.name = name
        self.last_write = None
        self.readers = []


class OpRec:
    __slots__ = ("idx", "eng", "fn", "deps", "needs_inc", "inc_val", "waits",
                 "drain_upto", "label")

    def __init__(self, idx, eng, fn, deps, label):
        self.idx = idx
        self.eng = eng
        self.fn = fn
        self.deps = deps
        self.needs_inc = False
        self.inc_val = None
        self.waits = {}
        self.drain_upto = -1
        self.label = label


class Sched:

    def __init__(self):
        self.ops = []
        self.last_dma = {}

    def op(self, eng, fn, reads=(), writes=(), label=""):
        reads = tuple(reads)
        writes = tuple(writes)
        excl = tuple(b for b in reads if b.name.startswith("ps") and b not in writes)
        reads = tuple(b for b in reads if not b.name.startswith("ps") or b in writes)
        writes = writes + excl
        deps = set()
        for b in reads:
            if b.last_write is not None:
                deps.add(b.last_write)
        for b in writes:
            if b.last_write is not None:
                deps.add(b.last_write)
            deps.update(b.readers)
        rec = OpRec(len(self.ops), eng, fn, deps, label)
        self.ops.append(rec)
        for b in reads:
            b.readers.append(rec.idx)
        for b in writes:
            b.last_write = rec.idx
            b.readers = []
        return rec

    def dma(self, queue, fn, reads=(), writes=(), label=""):
        rec = self.op(queue, fn, reads, writes, label)
        prev = self.last_dma.get(queue)
        if prev is not None:
            rec.deps.add(prev)
        self.last_dma[queue] = rec.idx
        return rec

    def finalize(self):
        for o in self.ops:
            for d in sorted(o.deps):
                dop = self.ops[d]
                if dop.eng == o.eng and dop.eng in ("dve", "act", "gpc"):
                    o.drain_upto = max(o.drain_upto, d)
                elif dop.eng == o.eng and dop.eng == "pe":
                    pass
                else:
                    dop.needs_inc = True
        ctr = {e: 0 for e in ENGINES}
        for o in self.ops:
            if o.eng in ("sp", "gp"):
                ctr[o.eng] += 16
                o.inc_val = ctr[o.eng]
                o.needs_inc = True
            elif o.needs_inc:
                ctr[o.eng] += 1
                o.inc_val = ctr[o.eng]
        for o in self.ops:
            for d in sorted(o.deps):
                dop = self.ops[d]
                if dop.eng != o.eng or dop.eng in ("sp", "gp"):
                    v = dop.inc_val
                    assert v is not None
                    if o.waits.get(dop.eng, 0) < v:
                        o.waits[dop.eng] = v
        return ctr

    def emit(self, nc, sems, handles):
        totals = {e: 0 for e in ENGINES}
        for o in self.ops:
            if o.inc_val is not None:
                totals[o.eng] = o.inc_val
        streams = {e: [i for i, o in enumerate(self.ops) if o.eng == e]
                   for e in ENGINES}

        def run_stream(eng_name, handle):
            names = (eng_name,) if isinstance(eng_name, str) else tuple(eng_name)
            idxs = sorted(i for n in names for i in streams[n])
            watermark = {e: 0 for e in ENGINES}
            last_drain = -1
            last_emitted = -1
            for i in idxs:
                o = self.ops[i]
                for e, v in sorted(o.waits.items()):
                    if watermark[e] < v:
                        handle.wait_ge(sems[e], v)
                        watermark[e] = v
                if o.drain_upto > last_drain and o.drain_upto <= last_emitted:
                    handle.drain()
                    last_drain = last_emitted
                instr = o.fn(handle)
                if o.label:
                    try:
                        instr.annotate(o.label)
                    except Exception:
                        pass
                last_emitted = i
                if o.inc_val is not None and o.eng not in ("sp", "gp"):
                    instr.then_inc(sems[o.eng], 1)
                elif o.eng in ("sp", "gp"):
                    instr.then_inc(sems[o.eng], 16)
            return watermark

        return streams, totals, run_stream


def ceil_div(a, b):
    return (a + b - 1) // b



def host_constants():
    consts = {}
    consts["iden128"] = np.eye(128, dtype=ml_dtypes.bfloat16)
    i64 = np.eye(64, dtype=ml_dtypes.bfloat16)
    consts["iden64x2"] = np.concatenate([i64, i64], axis=1)
    ar = np.arange(128)
    maskI = (ar[:, None] <= ar[None, :])
    maskS = (ar[:, None] < ar[None, :])
    consts["maskA"] = np.where(np.tile(maskI, (1, 4)), 0.0, NEG).astype(ml_dtypes.bfloat16)
    consts["maskM"] = np.where(maskS, 0.0, NEG).astype(ml_dtypes.bfloat16)
    selq = np.zeros((128, 8, 16), np.float32)
    for b in range(8):
        selq[0:64, b, 2 * b] = 1.0
        selq[64:128, b, 2 * b + 1] = 1.0
    consts["selq"] = selq.reshape(128, 128).astype(ml_dtypes.bfloat16)
    selk = np.zeros((128, 2, 4), np.float32)
    for b in range(2):
        selk[0:64, b, 2 * b] = 1.0
        selk[64:128, b, 2 * b + 1] = 1.0
    consts["selk"] = selk.reshape(128, 8).astype(ml_dtypes.bfloat16)
    Cm = np.zeros((128, 68), np.float32)
    for hk in range(4):
        Cm[hk, 0 + hk] = -1.0
        Cm[32 + hk, 0 + hk] = -0.5
        Cm[hk, 4 + hk] = 1.0
        Cm[68 + hk, 4 + hk] = 1.0
        Cm[32 + hk, 4 + hk] = -0.5
        Cm[68 + hk, 24 + hk] = 1.0
        Cm[hk, 64 + hk] = 1.0
        Cm[32 + hk, 64 + hk] = 0.5
    for h in range(16):
        Cm[h // 4, 8 + h] = 1.0
        Cm[96 + h, 8 + h] = -0.5
    consts["combC"] = Cm
    selr = np.zeros((28, 20, 128), np.float32)
    for r in range(20):
        selr[4 + r, r, :] = 1.0
    consts["selrow"] = selr.reshape(28, 20 * 128).astype(ml_dtypes.bfloat16)
    tr = np.zeros((96, 32), np.float32)
    for r in range(28):
        tr[r, r] = 1.0
    for r in range(4):
        tr[64 + r, 28 + r] = 1.0
    consts["trsel"] = tr
    consts["idenF4"] = np.eye(4, dtype=np.float32)
    selkd = np.zeros((128, 4, 4), np.float32)
    for hk in range(4):
        selkd[0:64, hk, hk] = 1.0
    consts["selkd"] = selkd.reshape(128, 16).astype(ml_dtypes.bfloat16)
    consts["ones1x512"] = np.ones((1, 512), ml_dtypes.bfloat16)
    cp = np.zeros((128, 12), np.float32)
    cp[:, 0] = 1e-5; cp[:, 1] = 1e-12; cp[:, 2] = 1.0; cp[:, 3] = 1.0 / 4096
    cp[:, 4] = -0.5; cp[:, 5] = -1.0; cp[:, 6] = 0.5; cp[:, 7] = np.log(0.125)
    cp[:, 8] = 1.0 / 1024; cp[:, 9] = -1.0 / 1024
    consts["cpool"] = cp
    consts["fones"] = np.ones((128, 128), np.float32)
    consts["onescol"] = np.ones((128, 1), ml_dtypes.bfloat16)
    return consts


def host_layout(inputs, core):
    m = {}
    m["x"] = np.ascontiguousarray(np.asarray(inputs["L0_attn_x"])[core])
    m["wqT"] = np.ascontiguousarray(np.asarray(inputs["L0_attn_Wq"]).T)
    wkT = np.asarray(inputs["L0_attn_Wk"]).T
    cols = []
    for hk in range(4):
        blk = wkT[:, hk * 64:(hk + 1) * 64]
        cols += [blk, blk]
    m["wkT"] = np.ascontiguousarray(np.concatenate(cols, axis=1))
    m["wvT"] = np.ascontiguousarray(np.asarray(inputs["L0_attn_Wv"]).T)
    m["wgT"] = np.ascontiguousarray(np.asarray(inputs["L0_attn_Wgate"]).T)
    m["wpT"] = np.ascontiguousarray(np.asarray(inputs["L0_attn_Wproj"]).T)
    m["wbgT"] = np.ascontiguousarray(
        np.concatenate([np.asarray(inputs["L0_attn_Wgk"]),
                        np.asarray(inputs["L0_attn_Wbeta"])], axis=0).T)
    m["bias_bg"] = np.ascontiguousarray(
        np.concatenate([np.asarray(inputs["L0_attn_dt_bias"]),
                        np.asarray(inputs["L0_attn_b_beta"])])[:, None])
    m["ffn_b"] = np.ascontiguousarray(np.asarray(inputs["L0_ffn_b"])[None, :])
    m["ffn_WT"] = np.ascontiguousarray(np.asarray(inputs["L0_ffn_W"]).T)
    m["lora_downT"] = np.ascontiguousarray(np.asarray(inputs["L0_ffn_lora_down"]).T)
    m["down_aux"] = np.ascontiguousarray(np.asarray(inputs["L0_ffn_down_aux"]))
    m["lora_upT"] = np.ascontiguousarray(np.asarray(inputs["L0_ffn_lora_up"]).T)
    m["up_auxT"] = np.ascontiguousarray(np.asarray(inputs["L0_ffn_up_aux"]).T)
    m["cwq"] = np.ascontiguousarray(np.asarray(inputs["L0_attn_cwq"]))
    cwk = np.asarray(inputs["L0_attn_cwk"])
    m["cwk"] = np.ascontiguousarray(
        np.concatenate(sum([[cwk[hk * 64:(hk + 1) * 64]] * 2 for hk in range(4)], []), axis=0))
    m["cwv"] = np.ascontiguousarray(np.asarray(inputs["L0_attn_cwv"]))
    m["A_log"] = np.ascontiguousarray(np.asarray(inputs["L0_attn_A_log"])[:, None])
    m["rms_w2"] = np.ascontiguousarray(np.tile(np.asarray(inputs["L0_attn_rms_w"]), 2)[:, None])
    for k, v in host_constants().items():
        m[k] = v
    return m



def build_program(T=8192, stage=99):
    nc = bass.Bass("TRN2", target_bir_lowering=False)
    NSEG = T // SEG
    NT = SEG // 128

    dr = {}
    def din(name, shape, dtype=F32):
        dr[name] = nc.dram_tensor(name, shape, dtype, kind="ExternalInput").ap()
        return dr[name]

    din("x", [T, C])
    for n, s in [("wqT", [C, C]), ("wkT", [C, 512]), ("wvT", [C, 256]),
                 ("wgT", [C, C]), ("wpT", [C, C]), ("wbgT", [C, 8]),
                 ("bias_bg", [8, 1]), ("ffn_b", [1, C]), ("ffn_WT", [C, C]),
                 ("lora_downT", [100, 4]), ("down_aux", [100, C]),
                 ("lora_upT", [4, 50]), ("up_auxT", [50, C]),
                 ("cwq", [C, 4]), ("cwk", [512, 4]), ("cwv", [256, 4]),
                 ("A_log", [4, 1]), ("rms_w2", [128, 1]),
                 ("combC", [128, 68]), ("fones", [128, 128]),
                 ("trsel", [96, 32]), ("idenF4", [4, 4]),
                 ("cpool", [128, 12])]:
        din(n, s)
    for n, s in [("iden128", [128, 128]), ("iden64x2", [64, 128]),
                 ("selq", [128, 128]), ("selk", [128, 8]),
                 ("ones1x512", [1, 512]), ("selrow", [28, 20 * 128]),
                 ("selkd", [128, 16]), ("maskA", [128, 512]), ("maskM", [128, 128]),
                 ("onescol", [128, 1])]:
        din(n, s, BF16)
    out_d = nc.dram_tensor("out", [T, C], F32, kind="ExternalOutput").ap()
    dbg_q = dbg_k = dbg_v = dbg_r = dbg_s = dbg_o = None
    if stage == 3:
        dbg_o = nc.dram_tensor("dbg_o", [T, C], F32, kind="ExternalOutput").ap()
    if stage == 2:
        NSEG_ = T // SEG
        dbg_q = nc.dram_tensor("dbg_q", [NSEG_ * 128, 8 * SEG], BF16, kind="ExternalOutput").ap()
        dbg_k = nc.dram_tensor("dbg_k", [NSEG_ * 128, 4 * SEG], BF16, kind="ExternalOutput").ap()
        dbg_v = nc.dram_tensor("dbg_v", [NSEG_ * 128, 2 * SEG], BF16, kind="ExternalOutput").ap()
        dbg_r = nc.dram_tensor("dbg_r", [NSEG_ * 68, SEG], F32, kind="ExternalOutput").ap()
        dbg_s = nc.dram_tensor("dbg_s", [NSEG_ * 128, 4 * C], BF16, kind="ExternalOutput").ap()

    import contextlib
    es = contextlib.ExitStack()
    S = Sched()

    def sb(name, shape, dtype=F32):
        return es.enter_context(nc.sbuf_tensor(name, shape, dtype))

    def ps(name, shape, dtype=F32):
        return es.enter_context(nc.psum_tensor(name, shape, dtype))

    with es:
        wfT = sb("wfT", [128, 8 * C], BF16)
        wqT = sb("wqT_s", [128, 8 * C], BF16)
        wgT = sb("wgT_s", [128, 8 * C], BF16)
        wpT = sb("wpT_s", [128, 8 * C], BF16)
        wkT = sb("wkT_s", [128, 8 * 512], BF16)
        wvT = sb("wvT_s", [128, 8 * 256], BF16)
        wbgT = sb("wbgT_s", [128, 8 * 8], BF16)
        bbg_col = sb("bbg_col", [8, 1], F32)
        bf_row = sb("bf_row", [1, C], BF16)
        iden = sb("iden", [128, 128], BF16)
        iden64x2 = sb("iden64x2_s", [64, 128], BF16)
        selq = sb("selq_s", [128, 128], BF16)
        ones1 = sb("ones1", [1, 512], BF16)
        maskA = sb("maskA_s", [128, 512], BF16)
        maskM = sb("maskM_s", [128, 128], BF16)
        combC = sb("combC_s", [128, 68], F32)
        trsel = sb("trsel_s", [96, 32], F32)
        idenF4 = sb("idenF4_s", [4, 4], F32)
        selrow = sb("selrow_s", [28, 20 * 128], BF16)
        selkd = sb("selkd_s", [128, 16], BF16)
        onescol = sb("onescol_s", [128, 1], BF16)
        cpool = sb("cpool_s", [128, 12], F32)
        fones = sb("fones_s", [128, 128], F32)
        cw = sb("cw", [128, 56], F32)
        negA = sb("negA", [36, 1], F32)
        rmsc = sb("rmsc", [128, 1], F32)

        xseg = [sb(f"xseg{i}", [128, NT * C], F32) for i in range(2)]
        hseg_ = [sb(f"hseg{i}", [128, C], BF16) for i in range(2)]
        hseg = hseg_[0]
        sqbuf = sb("sqbuf", [128, C], BF16)
        stats = sb("stats", [128, 16], F32)
        wbar = sb("wbar", [1, C], BF16)
        lnrb = [sb(f"lnrb{i}", [1, 256], BF16) for i in range(2)]
        lnrf = [sb(f"lnrf{i}", [1, 256], F32) for i in range(2)]
        hf_ = [sb(f"hfr{i}", [128, 8 * 128], BF16) for i in range(2)]
        hf = hf_[0]
        ogf = hf_[0]
        h2f = sb("h2f", [128, 8 * 512], BF16)
        qraw = sb("qraw", [128, 8 * (SEG + 3)], BF16)
        kraw = sb("kraw", [128, 4 * (SEG + 3)], BF16)
        vraw = sb("vraw", [128, 2 * (SEG + 3)], BF16)
        qfm = sb("qfm", [128, 8 * SEG], BF16)
        kfm = sb("kfm", [128, 4 * SEG], BF16)
        vfm = sb("vfm", [128, 2 * SEG], BF16)
        srows = sb("srows", [128, 2 * SEG], F32)
        bgrows = srows[0:8, 0:SEG]
        ab8 = srows[0:8, SEG:2 * SEG]
        srcrows = sb("srcrows", [128, SEG], F32)
        rows2 = sb("rows2", [68, SEG], F32)
        rows2h = sb("rows2h", [28, SEG], BF16)
        rows2l = sb("rows2l", [28, SEG], BF16)
        epack_ = [sb("epack", [96, 128], F32)] * 2
        colpk_ = [sb(f"colpk{i}", [128, 32], F32) for i in range(2)]
        gLrow_ = [sb(f"gLrow{i}", [1, 4], BF16) for i in range(2)]
        gLbc_ = [sb(f"gLbc{i}", [128, 4], F32) for i in range(2)]
        Mpow_ = [sb(f"Mpow{i}", [128, 11 * 128], BF16) for i in range(2)]
        Ttb_ = [sb(f"Ttb{i}", [128, 128], BF16) for i in range(2)]
        P0_ = [sb(f"P0{i}", [128, 128], BF16) for i in range(2)]
        Atb_ = [sb(f"Atb{i}", [128, 512], BF16) for i in range(2)]
        decb_ = [sb("decb", [128, 512], BF16)] * 2
        Mtb_ = [sb(f"Mtb{i}", [128, 128], BF16) for i in range(2)]
        Ub_ = [sb(f"Ub{i}", [128, 64], BF16) for i in range(2)]
        Wb_ = [sb(f"Wb{i}", [128, 64], BF16) for i in range(2)]
        Rw_ = [sb(f"Rw{i}", [128, 64], F32) for i in range(2)]
        Pc_ = [[sb(f"Pc{i}_0", [128, 128], BF16), P0_[i]] for i in range(2)]
        ktm2 = sb("ktm2", [128, 128], BF16)
        Sf32 = sb("Sf32", [128, 4 * 64], F32)
        Sbf = sb("Sbf", [128, 4 * 64], BF16)
        ointer = sb("ointer", [128, 256], BF16)
        obuf = sb("obuf", [128, C], BF16)
        oinv = sb("oinv", [128, 16], F32)
        gbuf = sb("gbuf", [128, C], BF16)
        sgate_ = [sb(f"sgate{i}", [128, C], BF16) for i in range(2)]
        ebuf = sb("ebuf", [128, C], BF16)
        ogbuf = sb("ogbuf", [128, C], BF16)
        psA = ps("psA", [128, 512])
        psB = ps("psB", [128, 512])
        psC = ps("psC", [128, 512])
        psD = ps("psD", [128, 512])
        psE = ps("psE", [128, 512])
        psF = ps("psF", [128, 512])
        psG = ps("psG", [128, 512])
        psH = ps("psH", [128, 512])

        sems = {}
        for e in ENGINES:
            sems[e] = es.enter_context(nc.semaphore(f"sem_{e}"))

        B = {}
        def buf(name):
            if name not in B:
                B[name] = Buf(name)
            return B[name]

        def cast_dma(dst_ap, src_ap, wbuf, label):
            S.dma("gp", lambda g, d=dst_ap, s=src_ap: g.dma_start(out=d, in_=s),
                  writes=(wbuf,), label=label)

        for name, sbuf_t, width in [("wqT", wqT, C), ("wgT", wgT, C),
                                    ("wkT", wkT, 512), ("wvT", wvT, 256),
                                    ("wbgT", wbgT, 8)]:
            srcap = dr[name].rearrange("(b p) w -> p b w", p=128)
            dstap = sbuf_t[:, :].rearrange("p (b w) -> p b w", b=8)
            cast_dma(dstap, srcap, buf(name), f"load_{name}")
        S.dma("sp", lambda s_: s_.dma_start(out=bbg_col[:, :], in_=dr["bias_bg"][:, :]),
              writes=(buf("bbg"),), label="load_bbg")
        for name, sbuf_t in [("iden128", iden), ("iden64x2", iden64x2),
                             ("selq", selq), ("ones1x512", ones1),
                             ("selrow", selrow), ("selkd", selkd), ("onescol", onescol)]:
            S.dma("sp", lambda s_, d=sbuf_t, src=dr[name]: s_.dma_start(out=d[:, :], in_=src[:, :]),
                  writes=(buf(name),), label=f"load_{name}")
        for name, sbuf_t in [("maskA", maskA), ("maskM", maskM),
                             ("combC", combC),
                             ("cpool", cpool), ("trsel", trsel), ("idenF4", idenF4), ("fones", fones),
                             ("A_log", None), ("rms_w2", rmsc)]:
            if name == "A_log":
                S.dma("sp", lambda s_: s_.dma_start(out=negA[32:36, :], in_=dr["A_log"][:, :]),
                      writes=(buf("negA"),), label="load_Alog")
            else:
                S.dma("sp", lambda s_, d=sbuf_t, src=dr[name]: s_.dma_start(out=d[:, :], in_=src[:, :]),
                      writes=(buf(name),), label=f"load_{name}")
        S.dma("sp", lambda s_: s_.dma_start(
            out=cw[:, 0:32].rearrange("p (b k) -> p b k", b=8), in_=dr["cwq"].rearrange("(b p) k -> p b k", p=128)),
            writes=(buf("cw"),), label="load_cwq")
        S.dma("sp", lambda s_: s_.dma_start(
            out=cw[:, 32:48].rearrange("p (b k) -> p b k", b=4), in_=dr["cwk"].rearrange("(b p) k -> p b k", p=128)),
            writes=(buf("cw"),), label="load_cwk")
        S.dma("sp", lambda s_: s_.dma_start(
            out=cw[:, 48:56].rearrange("p (b k) -> p b k", b=2), in_=dr["cwv"].rearrange("(b p) k -> p b k", p=128)),
            writes=(buf("cw"),), label="load_cwv")
        S.op("act", lambda a: a.activation(out=negA[32:36, :], in_=negA[32:36, :], func=Act.Exp),
             reads=(buf("negA"),), writes=(buf("negA"),), label="negA_exp")
        S.op("dve", lambda v: v.tensor_scalar_mul(out=negA[32:36, :], in0=negA[32:36, :], scalar1=-1.0),
             reads=(buf("negA"),), writes=(buf("negA"),), label="negA_neg")

        lt1 = qfm[0:100, 0:1024]
        lt2 = qfm[0:100, 1024:1028]
        lt3 = qfm[0:4, 2048:3072]
        lt4 = qfm[0:4, 1028:1078]
        lt5 = ogbuf[0:50, 0:1024]
        lt6 = qfm[0:50, 3072:4096]
        cast_dma(lt1, dr["down_aux"][:, :], buf("qfm"), "load_downaux")
        cast_dma(lt2, dr["lora_downT"][:, :], buf("qfm"), "load_ldT")
        cast_dma(lt4, dr["lora_upT"][:, :], buf("qfm"), "load_luT")
        cast_dma(lt6, dr["up_auxT"][:, :], buf("qfm"), "load_uaT")
        for nh in range(2):
            S.op("pe", lambda t, nh=nh: t.matmul(
                out=psH[0:4, 0:512], lhsT=lt2,
                rhs=lt1[:, nh * 512:(nh + 1) * 512], start=True, stop=True),
                reads=(buf("qfm"), buf("qfm")), writes=(buf("psH"),), label="l1mm")
            S.op("dve", lambda v, nh=nh: v.tensor_copy(
                out=lt3[:, nh * 512:(nh + 1) * 512], in_=psH[0:4, 0:512]),
                reads=(buf("psH"),), writes=(buf("qfm"),), label="l1evac")
        for nh in range(2):
            S.op("pe", lambda t, nh=nh: t.matmul(
                out=psH[0:50, 0:512], lhsT=lt4,
                rhs=lt3[:, nh * 512:(nh + 1) * 512], start=True, stop=True),
                reads=(buf("qfm"), buf("qfm")), writes=(buf("psH"),), label="l2mm")
            S.op("dve", lambda v, nh=nh: v.tensor_copy(
                out=lt5[:, nh * 512:(nh + 1) * 512], in_=psH[0:50, 0:512]),
                reads=(buf("psH"),), writes=(buf("ogbuf"),), label="l2evac")
        for cb in range(8):
            wt_scratch = xseg[0]
            S.dma("sp", lambda s_, cb=cb: s_.dma_start(
                out=wt_scratch[:, 0:1024], in_=dr["ffn_WT"][cb * 128:(cb + 1) * 128, :]),
                writes=(buf("xseg0"),), label=f"load_WT{cb}")
            for nh in range(2):
                S.op("pe", lambda t, cb=cb, nh=nh: t.matmul(
                    out=(psF if nh == 0 else psG)[:, 0:512],
                    lhsT=lt5[:, cb * 128:(cb + 1) * 128],
                    rhs=lt6[:, nh * 512:(nh + 1) * 512], start=True, stop=True),
                    reads=(buf("ogbuf"), buf("qfm")), writes=(buf("psF") if nh == 0 else buf("psG"),),
                    label=f"l3mm{cb}_{nh}")
            for nh in range(2):
                S.op("dve", lambda v, cb=cb, nh=nh: v.scalar_tensor_tensor(
                    out=wfT[:, cb * 1024 + nh * 512: cb * 1024 + (nh + 1) * 512],
                    in0=wt_scratch[:, nh * 512:(nh + 1) * 512], scalar=1.0,
                    in1=(psF if nh == 0 else psG)[:, 0:512],
                    op0=Alu.mult, op1=Alu.add),
                    reads=(buf("xseg0"), buf("psF") if nh == 0 else buf("psG")), writes=(buf("wfT"),),
                    label=f"weff{cb}_{nh}")
        for cb in range(8):
            S.op("dve", lambda v, cb=cb: v.tensor_scalar_mul(
                out=wfT[:, cb * 1024:(cb + 1) * 1024],
                in0=wfT[:, cb * 1024:(cb + 1) * 1024], scalar1=0.5),
                reads=(buf("wfT"),), writes=(buf("wfT"),), label=f"wfT_half{cb}")
        S.dma("sp", lambda s_: s_.dma_start(out=xseg[0][0:1, 0:1024], in_=dr["ffn_b"][:, :]),
              writes=(buf("xseg0"),), label="load_ffnb")
        S.op("dve", lambda v: v.tensor_scalar_mul(out=bf_row[:, :], in0=xseg[0][0:1, 0:1024], scalar1=0.5),
             reads=(buf("xseg0"),), writes=(buf("bf_row"),), label="bf_half")
        for nh in range(2):
            for cb in range(8):
                S.op("pe", lambda t_, nh=nh, cb=cb: t_.matmul(
                    out=psH[0:1, 0:512], lhsT=onescol[:, :],
                    rhs=wfT[:, cb * 1024 + nh * 512: cb * 1024 + (nh + 1) * 512],
                    start=(cb == 0), stop=(cb == 7)),
                    reads=(buf("onescol"), buf("wfT")), writes=(buf("psH"),),
                    label=f"wbar_{nh}_{cb}")
            S.op("act", lambda a, nh=nh: a.activation(
                out=wbar[0:1, nh * 512:(nh + 1) * 512], in_=psH[0:1, 0:512],
                func=Act.Copy),
                reads=(buf("psH"),), writes=(buf("wbar"),), label=f"wbarev_{nh}")
        for cb in range(8):
            S.dma("sp", lambda s_, cb=cb: s_.dma_start(
                out=xseg[0][:, 0:1024], in_=dr["wpT"][cb * 128:(cb + 1) * 128, :]),
                writes=(buf("xseg0"),), label=f"load_wpT{cb}")
            S.op("dve", lambda v, cb=cb: v.tensor_scalar_mul(
                out=wpT[:, cb * 1024:(cb + 1) * 1024], in0=xseg[0][:, 0:1024],
                scalar1=rmsc[:, :]),
                reads=(buf("xseg0"), buf("rms_w2")), writes=(buf("wpT"),),
                label=f"wpT_fold{cb}")

        PAD = SEG + 3

        def tile_x(seg, t):
            return xseg[seg % 2][:, t * C:(t + 1) * C]

        def layernorm(seg, t, xb, tag):
            lp = t % 2
            hsg = hseg_[lp]
            bhs = f"hseg{lp}"
            cb0 = 2 + lp * 7
            def sc(i):
                return stats[:, cb0 + i:cb0 + i + 1]
            def sb_(i):
                return buf(f"lnst{lp}_{i}")
            xt = tile_x(seg, t)
            S.op("act", lambda a: a.activation(out=sqbuf[:, :], in_=xt, func=Act.Square),
                 reads=(xb,), writes=(buf("sqbuf"),), label=f"ln_sq_{tag}")
            S.op("dve", lambda v: v.tensor_reduce(
                out=sc(0), in_=xt, axis=Ax.X, op=Alu.add),
                reads=(xb,), writes=(sb_(0),), label=f"ln_sum_{tag}")
            S.op("dve", lambda v: v.tensor_reduce(
                out=sc(1), in_=sqbuf[:, :], axis=Ax.X, op=Alu.add),
                reads=(buf("sqbuf"),), writes=(sb_(1),), label=f"ln_ssq_{tag}")
            S.op("act", lambda a: a.activation(
                out=sc(2), in_=sc(0), func=Act.Copy, scale=cpool[:, 8:9]),
                reads=(sb_(0), buf("cpool")), writes=(sb_(2),), label=f"ln_mean_{tag}")
            S.op("act", lambda a: a.activation(
                out=sc(3), in_=sc(1), func=Act.Copy, scale=cpool[:, 9:10]),
                reads=(sb_(1), buf("cpool")), writes=(sb_(3),), label=f"ln_msq_{tag}")
            S.op("dve", lambda v: v.scalar_tensor_tensor(
                out=sc(4), in0=sc(2), scalar=sc(2),
                in1=sc(3), op0=Alu.mult, op1=Alu.add),
                reads=(sb_(2), sb_(3)), writes=(sb_(4),),
                label=f"ln_negvar_{tag}")
            S.op("act", lambda a: a.activation(
                out=sc(5), in_=sc(4), func=Act.Ln,
                scale=cpool[:, 5:6], bias=cpool[:, 0:1]),
                reads=(sb_(4), buf("cpool")), writes=(sb_(5),),
                label=f"ln_ln_{tag}")
            S.op("act", lambda a: a.activation(
                out=sc(6), in_=sc(5), func=Act.Exp,
                scale=cpool[:, 4:5]),
                reads=(sb_(5), buf("cpool")), writes=(sb_(6),),
                label=f"ln_inv_{tag}")
            S.op("dve", lambda v: v.tensor_scalar(
                out=hsg[:, :], in0=xt, scalar1=sc(2), scalar2=sc(6),
                op0=Alu.subtract, op1=Alu.mult),
                reads=(xb, sb_(2), sb_(6)), writes=(buf(bhs),),
                label=f"ln_norm_{tag}")
            return hsg, bhs

        def transpose_src(src_sb, src_buf, dst_ap_fn, dst_buf, tag):
            for half in range(2):
                for b in range(4):
                    cb = half * 4 + b
                    S.op("pe", lambda t_, cb=cb, b=b: t_.matmul(
                        out=psH[:, b * 128:(b + 1) * 128],
                        lhsT=src_sb[:, cb * 128:(cb + 1) * 128], rhs=iden[:, :],
                        start=True, stop=True),
                        reads=(src_buf, buf("iden128")), writes=(buf("psH"),),
                        label=f"tr_{tag}_{cb}")
                S.op("act", lambda a, half=half: a.activation(
                    out=dst_ap_fn(half), in_=psH[:, 0:512], func=Act.Copy),
                    reads=(buf("psH"),), writes=(dst_buf,),
                    label=f"trev_{tag}_{half}")

        def ffn_gen(seg, t, xb, tag, sps=None, spsb="psE", single=False):
            sps = psE if sps is None else sps
            p_ = t % 2
            hsegp, hfp = hseg_[p_], hf_[p_]
            bh, bf_ = f"hseg{p_}", f"hfr{p_}"
            lb, lf = lnrb[p_], lnrf[p_]
            blb, blf = f"lnrb{p_}", f"lnrf{p_}"
            xt = tile_x(seg, t)
            S.op("dve", lambda v: v.tensor_copy(out=hsegp[:, :], in_=xt),
                 reads=(xb,), writes=(buf(bh),), label=f"cast_{tag}_{seg}_{t}")
            yield
            transpose_src(hsegp, buf(bh),
                          lambda half: hfp[:, half * 512:(half + 1) * 512],
                          buf(bf_), f"{tag}{seg}_{t}")
            yield
            S.op("act", lambda a: a.activation(out=sqbuf[:, :], in_=hfp[:, :],
                                               func=Act.Square),
                 reads=(buf(bf_),), writes=(buf("sqbuf"),), label=f"fsq_{tag}_{seg}_{t}")
            for cb in range(8):
                S.op("pe", lambda t_, cb=cb: t_.matmul(
                    out=sps[0:1, 256:384], lhsT=onescol[:, :],
                    rhs=hfp[:, cb * 128:(cb + 1) * 128],
                    start=(cb == 0), stop=(cb == 7)),
                    reads=(buf(bf_), buf("onescol")), writes=(buf(spsb),),
                    label=f"fsum_{tag}_{seg}_{t}_{cb}")
            for cb in range(8):
                S.op("pe", lambda t_, cb=cb: t_.matmul(
                    out=sps[0:1, 384:512], lhsT=onescol[:, :],
                    rhs=sqbuf[:, cb * 128:(cb + 1) * 128],
                    start=(cb == 0), stop=(cb == 7)),
                    reads=(buf("sqbuf"), buf("onescol")), writes=(buf(spsb),),
                    label=f"fssq_{tag}_{seg}_{t}_{cb}")
            S.op("act", lambda a, lb=lb: a.activation(
                out=lb[0:1, 0:128], in_=sps[0:1, 256:384], func=Act.Copy,
                scale=cpool[0:1, 9:10]),
                reads=(buf(spsb), buf("cpool")), writes=(buf(blb),),
                label=f"fnegm_{tag}_{seg}_{t}")
            S.op("act", lambda a, lf=lf: a.activation(
                out=lf[0:1, 0:128], in_=sps[0:1, 384:512], func=Act.Copy,
                scale=cpool[0:1, 8:9]),
                reads=(buf(spsb), buf("cpool")), writes=(buf(blf),),
                label=f"fmsq_{tag}_{seg}_{t}")
            yield
            S.op("dve", lambda v, lb=lb, lf=lf: v.tensor_tensor(
                out=lf[0:1, 128:256], in0=lb[0:1, 0:128], in1=lb[0:1, 0:128],
                op=Alu.mult),
                reads=(buf(blb),), writes=(buf(blf),), label=f"fnm2_{tag}_{seg}_{t}")
            yield
            S.op("dve", lambda v, lf=lf: v.tensor_tensor(
                out=lf[0:1, 0:128], in0=lf[0:1, 0:128], in1=lf[0:1, 128:256],
                op=Alu.subtract),
                reads=(buf(blf),), writes=(buf(blf),), label=f"fvar_{tag}_{seg}_{t}")
            yield
            S.op("act", lambda a, lf=lf: a.activation(
                out=lf[0:1, 0:128], in_=lf[0:1, 0:128], func=Act.Ln,
                bias=cpool[0:1, 0:1]),
                reads=(buf(blf), buf("cpool")), writes=(buf(blf),),
                label=f"flnv_{tag}_{seg}_{t}")
            yield
            S.op("act", lambda a, lf=lf, lb=lb: a.activation(
                out=lb[0:1, 128:256], in_=lf[0:1, 0:128], func=Act.Exp,
                scale=cpool[0:1, 4:5]),
                reads=(buf(blf), buf("cpool")), writes=(buf(blb),),
                label=f"finv_{tag}_{seg}_{t}")
            yield
            S.op("pe", lambda t_, lb=lb: t_.matmul(
                out=sps[:, 240:241], lhsT=lb[0:1, 128:256], rhs=ones1[0:1, 0:1],
                start=True, stop=True),
                reads=(buf(blb), buf("ones1x512")), writes=(buf(spsb),),
                label=f"ficol_{tag}_{seg}_{t}")
            S.op("dve", lambda v: v.tensor_copy(
                out=stats[:, p_:p_ + 1], in_=sps[:, 240:241]),
                reads=(buf(spsb),), writes=(buf(f"st_{p_}"),),
                label=f"ficev_{tag}_{seg}_{t}")
            yield
            for nh in range(2):
                pp = psF if (single or nh == 0) else psG
                pb = buf("psF") if (single or nh == 0) else buf("psG")
                for cb in range(8):
                    S.op("pe", lambda t_, cb=cb, nh=nh, pp=pp: t_.matmul(
                        out=pp[:, 0:512], lhsT=hfp[:, cb * 128:(cb + 1) * 128],
                        rhs=wfT[:, cb * 1024 + nh * 512: cb * 1024 + (nh + 1) * 512],
                        start=(cb == 0), stop=False),
                        reads=(buf(bf_), buf("wfT")), writes=(pb,),
                        label=f"ffn_{tag}_{seg}_{t}_{nh}_{cb}")
                S.op("pe", lambda t_, nh=nh, pp=pp: t_.matmul(
                    out=pp[:, 0:512], lhsT=ones1[0:1, 0:128],
                    rhs=bf_row[:, nh * 512:(nh + 1) * 512], start=False, stop=False),
                    reads=(buf("ones1x512"), buf("bf_row")), writes=(pb,),
                    label=f"ffnb_{tag}_{seg}_{t}_{nh}")
                S.op("pe", lambda t_, nh=nh, pp=pp, lb=lb: t_.matmul(
                    out=pp[:, 0:512], lhsT=lb[0:1, 0:128],
                    rhs=wbar[0:1, nh * 512:(nh + 1) * 512], start=False, stop=True),
                    reads=(buf(blb), buf("wbar")), writes=(pb,),
                    label=f"ffnm_{tag}_{seg}_{t}_{nh}")
                S.op("dve", lambda v, nh=nh, pp=pp, xt=xt: v.scalar_tensor_tensor(
                    out=xt[:, nh * 512:(nh + 1) * 512], in0=pp[:, 0:512],
                    scalar=stats[:, p_:p_ + 1],
                    in1=xt[:, nh * 512:(nh + 1) * 512],
                    op0=Alu.mult, op1=Alu.add),
                    reads=(pb, xb, buf(f"st_{p_}")), writes=(xb,),
                    label=f"ffnev_{tag}_{seg}_{t}_{nh}")
            yield

        def run_pair(gens):
            alive = [True] * len(gens)
            while any(alive):
                for gi in range(len(gens)):
                    if alive[gi]:
                        try:
                            next(gens[gi])
                        except StopIteration:
                            alive[gi] = False

        for bname, t_sb in [("qraw", qraw), ("kraw", kraw), ("vraw", vraw)]:
            S.op("dve", lambda v, t_sb=t_sb: v.memset(t_sb[:, :], 0.0),
                 writes=(buf(bname),), label=f"memset_{bname}")
        S.op("dve", lambda v: v.memset(srcrows[:, :], 0.0),
             writes=(buf("srcrows"),), label="memset_srcrows")
        S.op("dve", lambda v: v.memset(epack_[0][:, :], 0.0),
             writes=(buf("epack"),), label="memset_epack")
        S.op("dve", lambda v: v.memset(Sf32[:, :], 0.0),
             writes=(buf("Sf32"),), label="memset_Sf32")
        S.op("dve", lambda v: v.memset(Sbf[:, :], 0.0),
             writes=(buf("Sbf"),), label="memset_Sbf")

        PROJ = [
            ("q", qraw, "qraw", qfm, "qfm", wqT, "wqT", 8, 1024, 0),
            ("k", kraw, "kraw", kfm, "kfm", wkT, "wkT", 4, 512, 32),
            ("v", vraw, "vraw", vfm, "vfm", wvT, "wvT", 2, 256, 48),
        ]

        xbufs = [buf("xseg0"), buf("xseg1")]
        src_x = dr["x"].rearrange("(s t p) c -> s p t c", p=128, t=NT)
        dst_x = out_d.rearrange("(s t p) c -> s p t c", p=128, t=NT)
        for seg in range(NSEG):
            xb = xbufs[seg % 2]
            if seg == 0 or stage == 1:
                S.dma("sp", lambda s_, seg=seg: s_.dma_start(
                    out=xseg[seg % 2][:, :].rearrange("p (t c) -> p t c", t=NT),
                    in_=src_x[seg]),
                    writes=(xb,), label=f"xload_{seg}")
            if seg == 0 or stage == 1:
                for tp in range(0, NT, 2):
                    run_pair([ffn_gen(seg, tp, xb, "f1"), ffn_gen(seg, tp + 1, xb, "f1")])
            if stage == 1:
                S.dma("gp", lambda g, seg=seg: g.dma_start(
                    out=dst_x[seg], in_=xseg[seg % 2][:, :].rearrange("p (t c) -> p t c", t=NT)),
                    reads=(xb,), label=f"xstore_{seg}")
                continue

            h2f_v = h2f[:, :].rearrange("p (cb t w) -> p cb t w", t=NT, w=128)
            lnq = []
            for t in range(NT):
                lnq.append(layernorm(seg, t, xb, f"l2_{seg}_{t}"))
                if t > 0:
                    hsg_, bhs_ = lnq[t - 1]
                    transpose_src(hsg_, buf(bhs_),
                                  lambda half, t=t - 1, h2f_v=h2f_v: h2f_v[:, half * 4:(half + 1) * 4, t, :],
                                  buf("h2f"), f"h2_{seg}_{t - 1}")
            hsg_, bhs_ = lnq[NT - 1]
            transpose_src(hsg_, buf(bhs_),
                          lambda half, t=NT - 1, h2f_v=h2f_v: h2f_v[:, half * 4:(half + 1) * 4, t, :],
                          buf("h2f"), f"h2_{seg}_{NT - 1}")

            for name, raw_sb, raw_b, _, _, _, _, nmb, _, _ in PROJ:
                for mb in range(nmb):
                    S.op("dve", lambda g, raw_sb=raw_sb, mb=mb: g.tensor_copy(
                        out=raw_sb[:, mb * PAD: mb * PAD + 3],
                        in_=raw_sb[:, mb * PAD + SEG: mb * PAD + SEG + 3]),
                        reads=(buf(raw_b),), writes=(buf(raw_b + "c"),),
                        label=f"carry_{name}_{seg}_{mb}")

            pbank = [0]

            def proj_gen(name, raw_sb, raw_b, fm_sb, fm_b, wT_sb, wT_b, nmb,
                         width, cwo, sel_sb, sel_b, nsel, ssq_ps, ssq_psb, ssq_rows):
                for mb in range(nmb):
                    bk = pbank[0] % 2
                    pbank[0] += 1
                    pp = psA if bk == 0 else psB
                    pbn = "psA" if bk == 0 else "psB"
                    for cb in range(8):
                        S.op("pe", lambda t_, cb=cb, mb=mb, pp=pp: t_.matmul(
                            out=pp[:, 0:512],
                            lhsT=wT_sb[:, cb * width + mb * 128: cb * width + (mb + 1) * 128],
                            rhs=h2f[:, cb * 512:(cb + 1) * 512],
                            start=(cb == 0), stop=(cb == 7)),
                            reads=(buf(wT_b), buf("h2f")), writes=(buf(pbn),),
                            label=f"proj_{name}_{seg}_{mb}_{cb}")
                    S.op("act", lambda a, mb=mb, pp=pp: a.activation(
                        out=raw_sb[:, mb * PAD + 3: mb * PAD + 3 + SEG],
                        in_=pp[:, 0:512], func=Act.Copy),
                        reads=(buf(pbn), buf(raw_b + "c")), writes=(buf(raw_b),),
                        label=f"projev_{name}_{seg}_{mb}")
                    yield
                    S.op("dve", lambda v, mb=mb: v.tensor_scalar_mul(
                        out=fm_sb[:, mb * SEG:(mb + 1) * SEG],
                        in0=raw_sb[:, mb * PAD: mb * PAD + SEG],
                        scalar1=cw[:, cwo + mb * 4: cwo + mb * 4 + 1]),
                        reads=(buf(raw_b), buf("cw")), writes=(buf(fm_b),),
                        label=f"conv_{name}_{seg}_{mb}_0")
                    for j in range(1, 4):
                        S.op("dve", lambda v, mb=mb, j=j: v.scalar_tensor_tensor(
                            out=fm_sb[:, mb * SEG:(mb + 1) * SEG],
                            in0=raw_sb[:, mb * PAD + j: mb * PAD + j + SEG],
                            scalar=cw[:, cwo + mb * 4 + j: cwo + mb * 4 + j + 1],
                            in1=fm_sb[:, mb * SEG:(mb + 1) * SEG],
                            op0=Alu.mult, op1=Alu.add),
                            reads=(buf(raw_b), buf("cw"), buf(fm_b)), writes=(buf(fm_b),),
                            label=f"conv_{name}_{seg}_{mb}_{j}")
                    yield
                    if sel_sb is not None and name == "q":
                        S.op("act", lambda a, mb=mb: a.activation(
                            out=hseg_[0][:, (mb % 2) * 512:(mb % 2) * 512 + 512],
                            in_=fm_sb[:, mb * SEG:(mb + 1) * SEG],
                            func=Act.Square),
                            reads=(buf(fm_b),), writes=(buf("hseg0"),),
                            label=f"ssq_{name}_{seg}_{mb}")
                        if mb > 0:
                            S.op("pe", lambda t_, mb=mb: t_.matmul(
                                out=ssq_ps, lhsT=sel_sb[:, (mb - 1) * nsel:mb * nsel],
                                rhs=hseg_[0][:, ((mb - 1) % 2) * 512:((mb - 1) % 2) * 512 + 512],
                                start=(mb - 1 == 0), stop=False),
                                reads=(buf("hseg0"), buf(sel_b)), writes=(buf(ssq_psb),),
                                label=f"ssqmm_{name}_{seg}_{mb - 1}")
                        yield
                        if mb == nmb - 1:
                            S.op("pe", lambda t_, mb=mb: t_.matmul(
                                out=ssq_ps, lhsT=sel_sb[:, mb * nsel:(mb + 1) * nsel],
                                rhs=hseg_[0][:, (mb % 2) * 512:(mb % 2) * 512 + 512],
                                start=False, stop=True),
                                reads=(buf("hseg0"), buf(sel_b)), writes=(buf(ssq_psb),),
                                label=f"ssqmm_{name}_{seg}_{mb}")
                            yield
                    elif sel_sb is not None:
                        S.op("act", lambda a, mb=mb: a.activation(
                            out=sqbuf[:, 0:512], in_=fm_sb[:, mb * SEG:(mb + 1) * SEG],
                            func=Act.Square),
                            reads=(buf(fm_b),), writes=(buf("sqbuf"),),
                            label=f"ssq_{name}_{seg}_{mb}")
                        S.op("pe", lambda t_, mb=mb: t_.matmul(
                            out=ssq_ps, lhsT=sel_sb[:, mb * nsel:(mb + 1) * nsel],
                            rhs=sqbuf[:, 0:512], start=(mb == 0), stop=(mb == nmb - 1)),
                            reads=(buf("sqbuf"), buf(sel_b)), writes=(buf(ssq_psb),),
                            label=f"ssqmm_{name}_{seg}_{mb}")
                        yield

            def bg_gen():
                for cb in range(8):
                    S.op("pe", lambda t_, cb=cb: t_.matmul(
                        out=psG[0:8, 0:512], lhsT=wbgT[:, cb * 8:(cb + 1) * 8],
                        rhs=h2f[:, cb * 512:(cb + 1) * 512], start=(cb == 0), stop=(cb == 7)),
                        reads=(buf("wbgT"), buf("h2f")), writes=(buf("psG"),),
                        label=f"proj_bg_{seg}_{cb}")
                S.op("dve", lambda v: v.tensor_scalar_add(
                    out=bgrows, in0=psG[0:8, 0:512], scalar1=bbg_col[:, :]),
                    reads=(buf("psG"), buf("bbg")), writes=(buf("srows"),),
                    label=f"bgev_{seg}")
                yield

            def rows_gen():
                yield
                S.op("act", lambda a: a.activation(
                    out=ab8, in_=bgrows, func=Act.Abs),
                    reads=(buf("srows"),), writes=(buf("srows"),), label=f"sp_abs_{seg}")
                S.op("act", lambda a: a.activation(
                    out=ab8, in_=ab8, func=Act.Exp, scale=cpool[0:8, 5:6]),
                    reads=(buf("srows"), buf("cpool")), writes=(buf("srows"),),
                    label=f"sp_exp_{seg}")
                S.op("act", lambda a: a.activation(
                    out=ab8, in_=ab8, func=Act.Ln, bias=cpool[0:8, 2:3]),
                    reads=(buf("srows"), buf("cpool")), writes=(buf("srows"),),
                    label=f"sp_ln_{seg}")
                yield
                S.op("dve", lambda v: v.scalar_tensor_tensor(
                    out=srows[32:36, 0:SEG], in0=srows[0:4, 0:SEG], scalar=0.0,
                    in1=srows[0:4, SEG:2 * SEG], op0=Alu.max, op1=Alu.add),
                    reads=(buf("srows"),), writes=(buf("srows"),),
                    label=f"sp_gk_{seg}")
                S.op("dve", lambda v: v.tensor_scalar_mul(
                    out=srows[32:36, SEG:2 * SEG], in0=srows[32:36, 0:SEG],
                    scalar1=negA[32:36, :]),
                    reads=(buf("srows"), buf("negA")), writes=(buf("srows"),),
                    label=f"gkv_{seg}")
                yield
                for ch in range(NT):
                    S.op("dve", lambda v, ch=ch: v.tensor_tensor_scan(
                        out=srcrows[0:4, ch * 128:(ch + 1) * 128], data0=fones[32:36, 0:128],
                        data1=srows[32:36, SEG + ch * 128:SEG + (ch + 1) * 128], initial=0.0,
                        op0=Alu.mult, op1=Alu.add),
                        reads=(buf("srows"), buf("fones")), writes=(buf("srcrows"),),
                        label=f"gscan_{seg}_{ch}")
                yield
                S.op("dve", lambda v: v.scalar_tensor_tensor(
                    out=srcrows[64:72, :], in0=srows[0:8, 0:SEG], scalar=0.0,
                    in1=srows[0:8, SEG:2 * SEG], op0=Alu.min, op1=Alu.subtract),
                    reads=(buf("srows"),), writes=(buf("srcrows"),),
                    label=f"lnB_{seg}")
                yield

            run_pair([
                proj_gen("q", qraw, "qraw", qfm, "qfm", wqT, "wqT", 8, C, 0,
                         selq, "selq", 16, psH[0:16, 0:512], "psH", None),
                proj_gen("k", kraw, "kraw", kfm, "kfm", wkT, "wkT", 4, 512, 32,
                         selkd, "selkd", 4, psF[0:4, 0:512], "psF", None),
                proj_gen("v", vraw, "vraw", vfm, "vfm", wvT, "wvT", 2, 256, 48,
                         None, None, 0, None, None, None),
                bg_gen(),
                rows_gen(),
            ])
            S.op("act", lambda a: a.activation(
                out=srcrows[96:112, :], in_=psH[0:16, 0:512], func=Act.Ln,
                bias=cpool[0:16, 1:2]),
                reads=(buf("psH"), buf("cpool")), writes=(buf("srcrows"),),
                label=f"lnQ_{seg}")
            S.op("act", lambda a: a.activation(
                out=srcrows[32:36, :], in_=psF[0:4, 0:512], func=Act.Ln,
                bias=cpool[0:4, 1:2]),
                reads=(buf("psF"), buf("cpool")), writes=(buf("srcrows"),),
                label=f"lnK_{seg}")

            S.op("pe", lambda t_: t_.matmul(
                out=psH[0:68, 0:512], lhsT=combC[:, :], rhs=srcrows[:, :],
                start=True, stop=True),
                reads=(buf("combC"), buf("srcrows")), writes=(buf("psH"),),
                label=f"comb_{seg}")
            S.op("dve", lambda v: v.tensor_copy(out=rows2[:, :], in_=psH[0:68, 0:512]),
                 reads=(buf("psH"),), writes=(buf("rows2"),), label=f"r2f_{seg}")
            S.op("act", lambda a: a.activation(out=rows2h[:, :], in_=psH[0:28, 0:512],
                                               func=Act.Copy),
                 reads=(buf("psH"),), writes=(buf("rows2h"),), label=f"r2h_{seg}")
            S.op("dve", lambda v: v.tensor_tensor(
                out=rows2l[:, :], in0=rows2[0:28, :], in1=rows2h[:, :], op=Alu.subtract),
                reads=(buf("rows2"), buf("rows2h")), writes=(buf("rows2l"),),
                label=f"r2l_{seg}")

            if stage == 2:
                S.dma("gp", lambda g, seg=seg: g.dma_start(
                    out=dbg_q.rearrange("(s p) w -> s p w", p=128)[seg], in_=qfm[:, :]),
                    reads=(buf("qfm"),), label=f"dbgq_{seg}")
                S.dma("gp", lambda g, seg=seg: g.dma_start(
                    out=dbg_k.rearrange("(s p) w -> s p w", p=128)[seg], in_=kfm[:, :]),
                    reads=(buf("kfm"),), label=f"dbgk_{seg}")
                S.dma("gp", lambda g, seg=seg: g.dma_start(
                    out=dbg_v.rearrange("(s p) w -> s p w", p=128)[seg], in_=vfm[:, :]),
                    reads=(buf("vfm"),), label=f"dbgv_{seg}")
                S.dma("gp", lambda g, seg=seg: g.dma_start(
                    out=dbg_r.rearrange("(s p) w -> s p w", p=68)[seg], in_=rows2[:, :]),
                    reads=(buf("rows2"),), label=f"dbgr_{seg}")
                S.dma("gp", lambda g, seg=seg: g.dma_start(
                    out=dbg_s.rearrange("(s p) w -> s p w", p=128)[seg], in_=sgate[:, :]),
                    reads=(buf("qfm"),), label=f"dbgs_{seg}")
                S.dma("gp", lambda g, seg=seg: g.dma_start(
                    out=dst_x[seg], in_=xseg[seg % 2][:, :].rearrange("p (t c) -> p t c", t=NT)),
                    reads=(xb,), label=f"xstore_{seg}")
                continue

            def gate_gen(ct):
                gp_ = ct % 2
                sg = sgate_[gp_]
                bsg = f"sgate{gp_}"
                for nh in range(2):
                    for cb in range(8):
                        S.op("pe", lambda t_, cb=cb, nh=nh, ct=ct: t_.matmul(
                            out=psF[:, 0:512], lhsT=h2f[:, cb * 512 + ct * 128: cb * 512 + (ct + 1) * 128],
                            rhs=wgT[:, cb * 1024 + nh * 512: cb * 1024 + (nh + 1) * 512],
                            start=(cb == 0), stop=(cb == 7)),
                            reads=(buf("h2f"), buf("wgT")), writes=(buf("psF"),),
                            label=f"gproj_{seg}_{ct}_{nh}_{cb}")
                    S.op("act", lambda a, nh=nh: a.activation(
                        out=ebuf[:, nh * 512:(nh + 1) * 512], in_=psF[:, 0:512],
                        func=Act.Exp, scale=cpool[:, 5:6]),
                        reads=(buf("psF"), buf("cpool")), writes=(buf("ebuf"),),
                        label=f"gexp_{seg}_{ct}_{nh}")
                    S.op("act", lambda a, nh=nh: a.activation(
                        out=gbuf[:, nh * 512:(nh + 1) * 512], in_=psF[:, 0:512],
                        func=Act.Copy),
                        reads=(buf("psF"),), writes=(buf("gbuf"),),
                        label=f"gcopy_{seg}_{ct}_{nh}")
                    yield
                S.op("act", lambda a: a.activation(
                    out=ebuf[:, :], in_=ebuf[:, :], func=Act.Ln, bias=cpool[:, 2:3]),
                    reads=(buf("ebuf"), buf("cpool")), writes=(buf("ebuf"),),
                    label=f"gln_{seg}_{ct}")
                yield
                S.op("act", lambda a: a.activation(
                    out=ebuf[:, :], in_=ebuf[:, :], func=Act.Exp, scale=cpool[:, 5:6]),
                    reads=(buf("ebuf"), buf("cpool")), writes=(buf("ebuf"),),
                    label=f"gsig_{seg}_{ct}")
                yield
                S.op("dve", lambda v, sg=sg: v.tensor_tensor(
                    out=sg[:, :], in0=gbuf[:, :], in1=ebuf[:, :], op=Alu.mult),
                    reads=(buf("gbuf"), buf("ebuf")), writes=(buf(bsg),),
                    label=f"gsilu_{seg}_{ct}")
                yield

            def tail_gen(c, seg=seg, xb=xb):
                    S.op("act", lambda a: a.activation(out=sqbuf[:, :], in_=obuf[:, :],
                                                               func=Act.Square),
                         reads=(buf("obuf"),), writes=(buf("sqbuf"),), label=f"osq_{seg}_{c}")
                    S.op("dve", lambda v: v.tensor_reduce(
                        out=oinv[:, 0:16], in_=sqbuf[:, :].rearrange("p (h e) -> p h e", h=16),
                        axis=Ax.X, op=Alu.add),
                        reads=(buf("sqbuf"),), writes=(buf("oinv"),), label=f"ored_{seg}_{c}")
                    yield
                    S.op("act", lambda a: a.activation(
                        out=oinv[:, 0:16], in_=oinv[:, 0:16], func=Act.Ln,
                        scale=cpool[:, 3:4], bias=cpool[:, 0:1]),
                        reads=(buf("oinv"), buf("cpool")), writes=(buf("oinv"),),
                        label=f"oln_{seg}_{c}")
                    yield
                    S.op("act", lambda a: a.activation(
                        out=oinv[:, 0:16], in_=oinv[:, 0:16], func=Act.Exp,
                        scale=cpool[:, 4:5], bias=cpool[:, 7:8]),
                        reads=(buf("oinv"), buf("cpool")), writes=(buf("oinv"),),
                        label=f"oexp_{seg}_{c}")
                    yield
                    S.op("dve", lambda v, c=c: v.tensor_tensor(
                        out=ogbuf[:, :], in0=obuf[:, :], in1=sgate_[c % 2][:, :],
                        op=Alu.mult),
                        reads=(buf("obuf"), buf(f"sgate{c % 2}")),
                        writes=(buf("ogbuf"),), label=f"og1_{seg}_{c}")
                    yield
                    S.op("dve", lambda v: v.tensor_tensor(
                        out=ogbuf[:, :].rearrange("p (h w) -> p h w", w=64),
                        in0=ogbuf[:, :].rearrange("p (h w) -> p h w", w=64),
                        in1=oinv[:, 0:16].rearrange("p h -> p h ()").broadcast_to([128, 16, 64]),
                        op=Alu.mult),
                        reads=(buf("ogbuf"), buf("oinv")),
                        writes=(buf("ogbuf"),), label=f"og2_{seg}_{c}")
                    yield
                    transpose_src(ogbuf, buf("ogbuf"),
                                  lambda half: ogf[:, half * 512:(half + 1) * 512],
                                  buf("hfr0"), f"og_{seg}_{c}")
                    xt = tile_x(seg, c)
                    for nh in range(2):
                        for cb in range(8):
                            S.op("pe", lambda t_, cb=cb, nh=nh: t_.matmul(
                                out=psF[:, 0:512], lhsT=ogf[:, cb * 128:(cb + 1) * 128],
                                rhs=wpT[:, cb * 1024 + nh * 512: cb * 1024 + (nh + 1) * 512],
                                start=(cb == 0), stop=(cb == 7)),
                                reads=(buf("hfr0"), buf("wpT")), writes=(buf("psF"),),
                                label=f"wp_{seg}_{c}_{nh}_{cb}")
                        S.op("dve", lambda v, nh=nh, xt=xt: v.tensor_tensor(
                            out=xt[:, nh * 512:(nh + 1) * 512],
                            in0=xt[:, nh * 512:(nh + 1) * 512], in1=psF[:, 0:512], op=Alu.add),
                            reads=(buf("psF"), xb), writes=(xb,), label=f"wpev_{seg}_{c}_{nh}")
                        yield



            def cpre_gen(c):
                chs = slice(c * 128, (c + 1) * 128)
                ce = (c + 1) * 128
                cp_ = c % 2
                colpk = colpk_[cp_]; gLbc = gLbc_[cp_]
                gLrow = gLrow_[cp_]; epack = epack_[cp_]
                bcolpk = f"colpk{cp_}"; bgLbc = f"gLbc{cp_}"
                bgLrow = f"gLrow{cp_}"; bepack = "epack"
                S.op("pe", lambda t_, c=c, ce=ce: t_.matmul(
                    out=psH[0:1, 0:4], lhsT=srcrows[0:4, ce - 1:ce], rhs=idenF4[:, :],
                    start=True, stop=True),
                    reads=(buf("srcrows"), buf("idenF4")), writes=(buf("psH"),),
                    label=f"gLtr_{seg}_{c}")
                S.op("act", lambda a, gLrow=gLrow: a.activation(out=gLrow[:, :], in_=psH[0:1, 0:4],
                                                   func=Act.Exp),
                     reads=(buf("psH"),), writes=(buf(bgLrow),), label=f"gLexp_{seg}_{c}")
                S.op("pe", lambda t_, gLrow=gLrow: t_.matmul(
                    out=psH[:, 16:20], lhsT=ones1[0:1, 0:128], rhs=gLrow[:, :],
                    start=True, stop=True),
                    reads=(buf(bgLrow), buf("ones1x512")), writes=(buf("psH"),),
                    label=f"gLbc_{seg}_{c}")
                S.op("dve", lambda v, gLbc=gLbc: v.tensor_copy(out=gLbc[:, :], in_=psH[:, 16:20]),
                     reads=(buf("psH"),), writes=(buf(bgLbc),), label=f"gLev_{seg}_{c}")
                yield
                S.op("dve", lambda v, chs=chs, epack=epack: v.tensor_copy(
                    out=epack[0:28, :], in_=rows2[0:28, chs]),
                    reads=(buf("rows2"),), writes=(buf(bepack),), label=f"ep1_{seg}_{c}")
                S.op("dve", lambda v, ce=ce, epack=epack: v.tensor_scalar_add(
                    out=epack[0:4, :], in0=epack[0:4, :],
                    scalar1=srcrows[0:4, ce - 1:ce]),
                    reads=(buf(bepack), buf("srcrows")), writes=(buf(bepack),),
                    label=f"ep2_{seg}_{c}")
                S.op("dve", lambda v, chs=chs, epack=epack: v.tensor_copy(
                    out=epack[64:68, :], in_=rows2[64:68, chs]),
                    reads=(buf("rows2"),), writes=(buf(bepack),), label=f"ep3_{seg}_{c}")
                S.op("act", lambda a, epack=epack: a.activation(out=epack[0:28, :], in_=epack[0:28, :],
                                                   func=Act.Exp),
                     reads=(buf(bepack),), writes=(buf(bepack),), label=f"ep4_{seg}_{c}")
                yield
                S.op("pe", lambda t_, epack=epack: t_.matmul(
                    out=psH[:, 32:64], lhsT=epack[:, :], rhs=trsel[:, :],
                    start=True, stop=True),
                    reads=(buf(bepack), buf("trsel")), writes=(buf("psH"),),
                    label=f"eptr_{seg}_{c}")
                S.op("dve", lambda v, colpk=colpk: v.tensor_copy(out=colpk[:, :], in_=psH[:, 32:64]),
                     reads=(buf("psH"),), writes=(buf(bcolpk),), label=f"cpev_{seg}_{c}")
                yield


            qfm_r = qfm[:, :].rearrange("p (b t w) -> p b t w", t=NT, w=128)

            def pre_gen(hk, c):
                chs = slice(c * 128, (c + 1) * 128)
                cp_ = c % 2
                colpk = colpk_[cp_]; gLbc = gLbc_[cp_]
                bcolpk = f"colpk{cp_}"; bgLbc = f"gLbc{cp_}"
                p = hk % 2
                Mtb = Mtb_[p]; Atb = Atb_[p]; decb = decb_[p]
                Mpow = Mpow_[p]; Ttb = Ttb_[p]; P0 = P0_[p]
                Pc = Pc_[p]; Ub = Ub_[p]; Wb = Wb_[p]; Rw = Rw_[p]
                pow_ps = psC if p == 0 else psG
                pow_b = "psC" if p == 0 else "psG"
                bMtb = f"Mtb{p}"; bAtb = f"Atb{p}"; bdecb = "decb"
                bMpow = f"Mpow{p}"; bTtb = f"Ttb{p}"; bP0 = f"P0{p}"
                bUb = f"Ub{p}"; bWb = f"Wb{p}"; bRw = f"Rw{p}"
                kcols = slice(hk * SEG + c * 128, hk * SEG + (c + 1) * 128)
                kslA = kfm[0:64, kcols]
                kslB = kfm[64:128, kcols]
                order = [4 * hk, 4 * hk + 2, 4 * hk + 1, 4 * hk + 3]
                S.op("pe", lambda t_, kslA=kslA: t_.matmul(
                    out=psB[:, 128:256], lhsT=kslA, rhs=kslA, start=True, stop=True,
                    tile_position=(0, 0)),
                    reads=(buf("kfm"),), writes=(buf("psB"),),
                    label=f"gKK_{seg}_{c}_{hk}")
                for hl, rsrc, rb in ((0, rows2h, "rows2h"), (1, rows2l, "rows2l")):
                    S.op("pe", lambda t_, rsrc=rsrc, hl=hl: t_.matmul(
                        out=psB[:, 0:128], lhsT=selrow[:, hk * 128:(hk + 1) * 128],
                        rhs=rsrc[:, chs], start=(hl == 0), stop=(hl == 1)),
                        reads=(buf("selrow"), buf(rb)), writes=(buf("psB"),),
                        label=f"zm_{seg}_{c}_{hk}_{hl}")
                S.op("dve", lambda v: v.scalar_tensor_tensor(
                    out=psB[:, 0:128], in0=psB[:, 0:128],
                    scalar=colpk[:, 28 + hk:29 + hk], in1=maskM[:, :],
                    op0=Alu.subtract, op1=Alu.add),
                    reads=(buf("psB"), buf(bcolpk), buf("maskM")),
                    writes=(buf("psB"),), label=f"zms_{seg}_{c}_{hk}")
                S.op("act", lambda a: a.activation(out=decb[:, 0:128], in_=psB[:, 0:128],
                                                   func=Act.Exp),
                     reads=(buf("psB"),), writes=(buf(bdecb),),
                     label=f"zme_{seg}_{c}_{hk}")
                S.op("dve", lambda v: v.tensor_tensor(
                    out=Mtb[:, :], in0=psB[:, 128:256], in1=decb[:, 0:128], op=Alu.mult),
                    reads=(buf("psB"), buf(bdecb)), writes=(buf(bMtb),),
                    label=f"mt_{seg}_{c}_{hk}")
                yield
                yield
                S.op("pe", lambda t_: t_.matmul(
                    out=pow_ps[:, 0:128], lhsT=Mtb[:, :], rhs=iden[:, :],
                    start=True, stop=True),
                    reads=(buf(bMtb), buf("iden128")), writes=(buf(pow_b),),
                    label=f"m1_{seg}_{c}_{hk}")
                yield
                S.op("dve", lambda v: v.tensor_copy(
                    out=Mpow[:, 0:128], in_=pow_ps[:, 0:128]),
                    reads=(buf(pow_b),), writes=(buf(bMpow),),
                    label=f"m1ev_{seg}_{c}_{hk}")
                yield
                prevM, prevMt = 0, None
                slot = 1
                for lev in range(5):
                    mtprev = Mtb[:, :] if prevMt is None else Mpow[:, prevMt * 128:(prevMt + 1) * 128]
                    mprev = Mpow[:, prevM * 128:(prevM + 1) * 128]
                    S.op("pe", lambda t_, mtprev=mtprev, mprev=mprev: t_.matmul(
                        out=pow_ps[:, 0:128], lhsT=mtprev, rhs=mprev,
                        start=True, stop=True),
                        reads=(buf(bMpow), buf(bMtb)), writes=(buf(pow_b),),
                        label=f"sqm_{seg}_{c}_{hk}_{lev}")
                    S.op("pe", lambda t_, mtprev=mtprev, mprev=mprev: t_.matmul(
                        out=pow_ps[:, 128:256], lhsT=mprev, rhs=mtprev,
                        start=True, stop=True),
                        reads=(buf(bMpow), buf(bMtb)), writes=(buf(pow_b),),
                        label=f"sqt_{seg}_{c}_{hk}_{lev}")
                    yield
                    S.op("act", lambda a, slot=slot: a.activation(
                        out=Mpow[:, slot * 128:(slot + 2) * 128], in_=pow_ps[:, 0:256],
                        func=Act.Copy),
                        reads=(buf(pow_b),), writes=(buf(bMpow),),
                        label=f"sqev_{seg}_{c}_{hk}_{lev}")
                    yield
                    prevM, prevMt = slot, slot + 1
                    slot += 2
                S.op("pe", lambda t_, prevM=prevM, prevMt=prevMt: t_.matmul(
                    out=pow_ps[:, 0:128], lhsT=Mpow[:, prevMt * 128:(prevMt + 1) * 128],
                    rhs=Mpow[:, prevM * 128:(prevM + 1) * 128], start=True, stop=True),
                    reads=(buf(bMpow),), writes=(buf(pow_b),),
                    label=f"m64_{seg}_{c}_{hk}")
                yield
                S.op("act", lambda a: a.activation(
                    out=Mpow[:, 0:128], in_=pow_ps[:, 0:128], func=Act.Copy),
                    reads=(buf(pow_b),), writes=(buf(bMpow),),
                    label=f"m64ev_{seg}_{c}_{hk}")
                yield
                S.op("dve", lambda v: v.scalar_tensor_tensor(
                    out=P0[:, :], in0=Mtb[:, :], scalar=-1.0, in1=iden[:, :],
                    op0=Alu.mult, op1=Alu.add),
                    reads=(buf(bMtb), buf("iden128")), writes=(buf(bP0),),
                    label=f"p0_{seg}_{c}_{hk}")
                yield
                mslots = {2: 1, 4: 3, 8: 5, 16: 7, 32: 9, 64: 0}
                cur = P0
                curb = bP0
                for pi, pw in enumerate([2, 4, 8, 16, 32, 64]):
                    S.op("pe", lambda t_, pw=pw, cur=cur: t_.matmul(
                        out=pow_ps[:, 256:384], lhsT=Mpow[:, mslots[pw] * 128:(mslots[pw] + 1) * 128],
                        rhs=cur[:, 0:128], start=True, stop=True),
                        reads=(buf(bMpow), buf(curb)), writes=(buf(pow_b),),
                        label=f"tt_{seg}_{c}_{hk}_{pw}")
                    yield
                    if pw == 64:
                        nxt, nxtb = Ttb, bTtb
                    else:
                        nxt, nxtb = Pc[pi % 2], f"Pc{p}_{pi % 2}"
                    S.op("dve", lambda v, nxt=nxt, cur=cur: v.tensor_tensor(
                        out=nxt[:, 0:128], in0=cur[:, 0:128], in1=pow_ps[:, 256:384],
                        op=Alu.add),
                        reads=(buf(pow_b), buf(curb)), writes=(buf(nxtb),),
                        label=f"ttev_{seg}_{c}_{hk}_{pw}")
                    yield
                    cur, curb = nxt, nxtb
                S.op("pe", lambda t_, kslA=kslA: t_.matmul(
                    out=psA[:, 0:256], lhsT=kslA,
                    rhs=qfm_r[0:64, 2 * hk:2 * hk + 2, c, :],
                    start=True, stop=True, tile_position=(0, 0)),
                    reads=(buf("kfm"), buf("qfm")), writes=(buf("psA"),),
                    label=f"gKQa_{seg}_{c}_{hk}")
                S.op("pe", lambda t_, kslB=kslB: t_.matmul(
                    out=psD[:, 128:384], lhsT=kslB,
                    rhs=qfm_r[64:128, 2 * hk:2 * hk + 2, c, :],
                    start=True, stop=True, tile_position=(64, 0)),
                    reads=(buf("kfm"), buf("qfm")), writes=(buf("psD"),),
                    label=f"gKQb_{seg}_{c}_{hk}")
                for s in range(4):
                    h = order[s]
                    for hl, rsrc, rb in ((0, rows2h, "rows2h"), (1, rows2l, "rows2l")):
                        S.op("pe", lambda t_, s=s, h=h, rsrc=rsrc, hl=hl: t_.matmul(
                            out=psB[:, s * 128:(s + 1) * 128],
                            lhsT=selrow[:, (4 + h) * 128:(5 + h) * 128],
                            rhs=rsrc[:, chs], start=(hl == 0), stop=(hl == 1)),
                            reads=(buf("selrow"), buf(rb)), writes=(buf("psB"),),
                            label=f"z4_{seg}_{c}_{hk}_{s}_{hl}")
                S.op("dve", lambda v: v.scalar_tensor_tensor(
                    out=psB[:, 0:512], in0=psB[:, 0:512],
                    scalar=colpk[:, 28 + hk:29 + hk], in1=maskA[:, :],
                    op0=Alu.subtract, op1=Alu.add),
                    reads=(buf("psB"), buf(bcolpk), buf("maskA")),
                    writes=(buf("psB"),), label=f"z4s_{seg}_{c}_{hk}")
                S.op("act", lambda a: a.activation(out=decb[:, :], in_=psB[:, 0:512],
                                                   func=Act.Exp),
                     reads=(buf("psB"),), writes=(buf(bdecb),),
                     label=f"z4e_{seg}_{c}_{hk}")
                S.op("dve", lambda v: v.tensor_tensor(
                    out=Atb[:, 0:256], in0=psA[:, 0:256], in1=decb[:, 0:256], op=Alu.mult),
                    reads=(buf("psA"), buf(bdecb)), writes=(buf(bAtb),),
                    label=f"at_{seg}_{c}_{hk}")
                S.op("dve", lambda v: v.tensor_tensor(
                    out=Atb[:, 256:512], in0=psD[:, 128:384], in1=decb[:, 256:512], op=Alu.mult),
                    reads=(buf("psD"), buf(bdecb)), writes=(buf(bAtb),),
                    label=f"at2_{seg}_{c}_{hk}")
                yield

            def seq_gen(hk, c):
                chs = slice(c * 128, (c + 1) * 128)
                cp_ = c % 2
                colpk = colpk_[cp_]; gLbc = gLbc_[cp_]
                bcolpk = f"colpk{cp_}"; bgLbc = f"gLbc{cp_}"
                p = hk % 2
                Mtb = Mtb_[p]; Atb = Atb_[p]; decb = decb_[p]
                Mpow = Mpow_[p]; Ttb = Ttb_[p]; P0 = P0_[p]
                Pc = Pc_[p]; Ub = Ub_[p]; Wb = Wb_[p]; Rw = Rw_[p]
                pow_ps = psC if p == 0 else psG
                pow_b = "psC" if p == 0 else "psG"
                bMtb = f"Mtb{p}"; bAtb = f"Atb{p}"; bdecb = "decb"
                bMpow = f"Mpow{p}"; bTtb = f"Ttb{p}"; bP0 = f"P0{p}"
                bUb = f"Ub{p}"; bWb = f"Wb{p}"; bRw = f"Rw{p}"
                kcols = slice(hk * SEG + c * 128, hk * SEG + (c + 1) * 128)
                kslA = kfm[0:64, kcols]
                kslB = kfm[64:128, kcols]
                order = [4 * hk, 4 * hk + 2, 4 * hk + 1, 4 * hk + 3]
                S.op("pe", lambda t_, kslA=kslA: t_.matmul(
                    out=psD[:, 256:320], lhsT=kslA,
                    rhs=Sbf[0:64, hk * 64:(hk + 1) * 64], start=True, stop=True,
                    tile_position=(0, 0)),
                    reads=(buf("kfm"), buf("Sbf")), writes=(buf("psD"),),
                    label=f"rmm_{seg}_{c}_{hk}")
                S.op("dve", lambda v: v.tensor_scalar_mul(
                    out=Rw[:, :], in0=psD[:, 256:320], scalar1=colpk[:, 4 + hk:5 + hk]),
                    reads=(buf("psD"), buf(bcolpk)), writes=(buf(bRw),),
                    label=f"rw_{seg}_{c}_{hk}")
                yield
                vb = hk // 2
                vbase = (hk % 2) * 64
                S.op("pe", lambda t_, vb=vb, vbase=vbase: t_.matmul(
                    out=psD[:, 320:384],
                    lhsT=vfm[vbase:vbase + 64, vb * SEG + c * 128: vb * SEG + (c + 1) * 128],
                    rhs=iden[vbase:vbase + 64, vbase:vbase + 64],
                    start=True, stop=True, tile_position=(vbase, 0)),
                    reads=(buf("vfm"), buf("iden128")), writes=(buf("psD"),),
                    label=f"vtm_{seg}_{c}_{hk}")
                S.op("dve", lambda v: v.scalar_tensor_tensor(
                    out=Wb[:, :], in0=psD[:, 320:384], scalar=colpk[:, 24 + hk:25 + hk],
                    in1=Rw[:, :], op0=Alu.mult, op1=Alu.subtract),
                    reads=(buf("psD"), buf(bcolpk), buf(bRw)), writes=(buf(bWb),),
                    label=f"wb_{seg}_{c}_{hk}")
                yield
                S.op("pe", lambda t_: t_.matmul(
                    out=psD[:, 0:64], lhsT=Ttb[:, :], rhs=Wb[:, :],
                    start=True, stop=True),
                    reads=(buf(bTtb), buf(bWb)), writes=(buf("psD"),),
                    label=f"umm_{seg}_{c}_{hk}")
                S.op("act", lambda a: a.activation(out=Ub[:, :], in_=psD[:, 0:64],
                                                   func=Act.Copy),
                     reads=(buf("psD"),), writes=(buf(bUb),),
                     label=f"uev_{seg}_{c}_{hk}")
                yield
                for s in range(4):
                    S.op("pe", lambda t_, s=s: t_.matmul(
                        out=psE[:, s * 64:(s + 1) * 64], lhsT=Atb[:, s * 128:(s + 1) * 128],
                        rhs=Ub[:, :], start=True, stop=True),
                        reads=(buf(bAtb), buf(bUb)), writes=(buf("psE"),),
                        label=f"oi_{seg}_{c}_{hk}_{s}")
                for s in range(4):
                    h = order[s]
                    qb_ = h // 2
                    qbase = (h % 2) * 64
                    outap = (psE[:, 256 + s * 64:256 + (s + 1) * 64] if qbase == 0
                             else psD[:, 128 + (s - 2) * 64:128 + (s - 1) * 64])
                    outb = buf("psE") if qbase == 0 else buf("psD")
                    S.op("pe", lambda t_, s=s, qb_=qb_, qbase=qbase, outap=outap: t_.matmul(
                        out=outap,
                        lhsT=qfm[qbase:qbase + 64, qb_ * SEG + c * 128: qb_ * SEG + (c + 1) * 128],
                        rhs=Sbf[qbase:qbase + 64, hk * 64:(hk + 1) * 64],
                        start=True, stop=True, tile_position=(qbase, 0)),
                        reads=(buf("qfm"), buf("Sbf")), writes=(outb,),
                        label=f"on_{seg}_{c}_{hk}_{s}")
                S.op("act", lambda a: a.activation(out=ointer[:, 0:128], in_=psE[:, 256:384],
                                                   func=Act.Copy),
                     reads=(buf("psE"),), writes=(buf("ointer"),),
                     label=f"onev_{seg}_{c}_{hk}")
                S.op("act", lambda a: a.activation(out=ointer[:, 128:256], in_=psD[:, 128:256],
                                                   func=Act.Copy),
                     reads=(buf("psD"),), writes=(buf("ointer"),),
                     label=f"onev2_{seg}_{c}_{hk}")
                for s in range(4):
                    h = order[s]
                    S.op("dve", lambda v, s=s, h=h: v.scalar_tensor_tensor(
                        out=obuf[:, h * 64:(h + 1) * 64], in0=ointer[:, s * 64:(s + 1) * 64],
                        scalar=colpk[:, 8 + h:9 + h], in1=psE[:, s * 64:(s + 1) * 64],
                        op0=Alu.mult, op1=Alu.add),
                        reads=(buf("ointer"), buf(bcolpk), buf("psE")),
                        writes=(buf("obuf"),), label=f"oc_{seg}_{c}_{hk}_{s}")
                yield
                S.op("pe", lambda t_, kslA=kslA: t_.matmul(
                    out=psD[:, 384:512], lhsT=kslA, rhs=iden64x2[:, :],
                    start=True, stop=True, tile_position=(0, 0)),
                    reads=(buf("kfm"), buf("iden64x2")), writes=(buf("psD"),),
                    label=f"ktm_{seg}_{c}_{hk}")
                S.op("dve", lambda v: v.tensor_scalar_mul(
                    out=ktm2[:, :], in0=psD[:, 384:512], scalar1=colpk[:, 0 + hk:1 + hk]),
                    reads=(buf("psD"), buf(bcolpk)), writes=(buf("ktm2"),),
                    label=f"khat_{seg}_{c}_{hk}")
                S.op("pe", lambda t_: t_.matmul(
                    out=psD[:, 64:128], lhsT=ktm2[:, :], rhs=Ub[:, :],
                    start=True, stop=True),
                    reads=(buf("ktm2"), buf(bUb)), writes=(buf("psD"),),
                    label=f"smm_{seg}_{c}_{hk}")
                S.op("dve", lambda v: v.scalar_tensor_tensor(
                    out=Sf32[:, hk * 64:(hk + 1) * 64], in0=Sf32[:, hk * 64:(hk + 1) * 64],
                    scalar=gLbc[:, hk:hk + 1], in1=psD[:, 64:128],
                    op0=Alu.mult, op1=Alu.add),
                    reads=(buf("Sf32"), buf(bgLbc), buf("psD")),
                    writes=(buf("Sf32"),), label=f"s1_{seg}_{c}_{hk}")
                yield
                S.op("gpc", lambda g: g.tensor_copy(
                    out=Sbf[:, hk * 64:(hk + 1) * 64], in_=Sf32[:, hk * 64:(hk + 1) * 64]),
                    reads=(buf("Sf32"),), writes=(buf("Sbf"),),
                    label=f"sbf_{seg}_{c}_{hk}")

            run_pair([cpre_gen(0), gate_gen(0)])
            run_pair([pre_gen(0, 0), pre_gen(1, 0)])
            for c in range(NT):
                g1 = [seq_gen(0, c), seq_gen(1, c), pre_gen(2, c), pre_gen(3, c)]
                if c > 0:
                    g1.append(ffn_gen(seg, c - 1, xb, "f2", sps=psH, spsb="psH", single=True))
                run_pair(g1)
                if c + 1 < NT:
                    run_pair([seq_gen(2, c), seq_gen(3, c), gate_gen(c + 1),
                              cpre_gen(c + 1)])
                    run_pair([tail_gen(c), pre_gen(0, c + 1), pre_gen(1, c + 1)])
                else:
                    run_pair([seq_gen(2, c), seq_gen(3, c)])
                    if seg + 1 < NSEG:
                        xbn = xbufs[(seg + 1) % 2]
                        S.dma("sp", lambda s_, seg=seg: s_.dma_start(
                            out=xseg[(seg + 1) % 2][:, :].rearrange("p (t c) -> p t c", t=NT),
                            in_=src_x[seg + 1]),
                            writes=(xbn,), label=f"xload_{seg + 1}")
                        run_pair([tail_gen(c), ffn_gen(seg + 1, 1, xbn, "f1")])
                    else:
                        run_pair([tail_gen(c)])
            g4 = [ffn_gen(seg, NT - 1, xb, "f2", sps=psH, spsb="psH", single=True)]
            if seg + 1 < NSEG:
                g4 += [ffn_gen(seg + 1, 0, xbn, "f1")]
            run_pair(g4)
            if seg + 1 < NSEG:
                run_pair([ffn_gen(seg + 1, 2, xbn, "f1"),
                          ffn_gen(seg + 1, 3, xbn, "f1")])
            S.dma("gp", lambda g, seg=seg: g.dma_start(
                out=dst_x[seg], in_=xseg[seg % 2][:, :].rearrange("p (t c) -> p t c", t=NT)),
                reads=(xb,), label=f"xstore_{seg}")
            continue

        order = S.finalize()
        streams, totals, run_stream = S.emit(nc, sems, None)
        with nc.Block() as block:
            @block.tensor
            def _(e):
                run_stream("pe", e)

            @block.vector
            def _(e):
                run_stream("dve", e)

            @block.scalar
            def _(e):
                run_stream("act", e)

            @block.sync
            def _(e):
                run_stream("sp", e)

            @block.gpsimd
            def _(e):
                run_stream(("gp", "gpc"), e)
                if totals["gp"]:
                    e.wait_ge(sems["gp"], totals["gp"])
                if totals["gpc"]:
                    e.wait_ge(sems["gpc"], totals["gpc"])
    return nc


def kernel(**inputs):
    T = np.asarray(inputs["L0_attn_x"]).shape[1]
    nc = build_program(T=T, stage=4)
    in_maps = [host_layout(inputs, core) for core in range(8)]
    res = run_bass_kernel_spmd(nc, in_maps, core_ids=list(range(8)))
    return np.stack([r["out"] for r in res.results]).astype(np.float32)
